# version 09
# baseline (speedup 1.0000x reference)
import contextlib
import numpy as np

import concourse.bass as bass
import concourse.mybir as mybir
from concourse.bass_utils import run_bass_kernel_spmd

F32 = mybir.dt.float32
BF16 = mybir.dt.bfloat16
AF = mybir.ActivationFunctionType
ALU = mybir.AluOpType
AXX = mybir.AxisListType.X

D = 1024
H = 16
HD = 64
SCALE = 8.0 / HD
EPS = 1e-5
NE = 8


class Plan:
    def __init__(self, strict_edges=False):
        self.ops = {"tensor": [], "vector": [], "scalar": [], "sync": [], "gpsimd": []}
        self.counts = {}
        self.strict_edges = strict_edges

    ENG_SEM = {"vector": "VEC", "scalar": "SCE", "gpsimd": "PLE"}

    def emit(self, engine, fn, waits=(), inc=None, inc_by=1):
        w = [(s, v) for (s, v) in waits if v and v > 0]
        if self.strict_edges:
            es_ = self.ENG_SEM.get(engine)
            if es_ is not None:
                pc = self.counts.get(es_, 0)
                if pc > 0 and (es_, pc) not in w:
                    w.append((es_, pc))
        if inc is not None:
            self.counts[inc] = self.counts.get(inc, 0) + inc_by
        self.ops[engine].append((fn, tuple(w), inc, inc_by))
        return self.counts.get(inc, 0) if inc is not None else None

    def count(self, sem):
        return self.counts.get(sem, 0)


def build_nc(ntok=4096, nch_a=8, npair=8, ne=NE, attn_only=False, strict_edges=True):
    NT = ntok // 128
    nc = bass.Bass("TRN2", target_bir_lowering=False)

    dx = nc.dram_tensor("x", [ntok, D], F32, kind="ExternalInput").ap()
    dwqkv = nc.dram_tensor("wqkv", [128, 24, 8, 128], F32, kind="ExternalInput").ap()
    dwproj = nc.dram_tensor("wproj", [128, 8, D], F32, kind="ExternalInput").ap()
    dw1 = nc.dram_tensor("w1", [ne, 128, 32, 8, 128], F32, kind="ExternalInput").ap()
    dw2 = nc.dram_tensor("w2", [ne, 128, 32, D], F32, kind="ExternalInput").ap()
    db1 = nc.dram_tensor("b1", [ne, 128, 32], F32, kind="ExternalInput").ap()
    db2 = nc.dram_tensor("b2", [ne, D], F32, kind="ExternalInput").ap()
    dgate = nc.dram_tensor("gate", [128, 8, ne], F32, kind="ExternalInput").ap()
    dgbc = nc.dram_tensor("gbc", [128, D], F32, kind="ExternalInput").ap()
    dbbc = nc.dram_tensor("bbc", [128, D], F32, kind="ExternalInput").ap()
    dpbc = nc.dram_tensor("pbc", [128, D], F32, kind="ExternalInput").ap()
    drot = nc.dram_tensor("rot", [128, 128], F32, kind="ExternalInput").ap()
    dones = nc.dram_tensor("onescol", [128, 1], F32, kind="ExternalInput").ap()
    donesr = nc.dram_tensor("onesrow", [1, 128], F32, kind="ExternalInput").ap()
    dout = nc.dram_tensor("out", [ntok, D], F32, kind="ExternalOutput").ap()

    swqkv = nc.dram_tensor("s_wqkv", [128, 24, 8, 128], BF16, kind="Internal").ap()
    swproj = nc.dram_tensor("s_wproj", [128, 8, D], BF16, kind="Internal").ap()
    sw1 = nc.dram_tensor("s_w1", [ne, 128, 32, 8, 128], BF16, kind="Internal").ap()
    sw2 = nc.dram_tensor("s_w2", [ne, 128, 32, D], BF16, kind="Internal").ap()

    es = contextlib.ExitStack()
    with es:
        def sbuf(shape, dt, _n=[0]):
            _n[0] += 1
            return es.enter_context(nc.sbuf_tensor(f"sb{_n[0]}", shape, dt))

        R = sbuf([128, NT, D], BF16)
        qkvB = sbuf([128, 32, 512], BF16)
        xln = sbuf([128, 4, D], BF16)
        xlnT = sbuf([128, 8, 512], BF16)
        WS = sbuf([128, 18432], BF16)
        FP = sbuf([128, 4, D], F32)
        outb = sbuf([128, 2, D], F32)
        wqr = sbuf([128, 2, 2, 8, 128], BF16)
        wpr = sbuf([128, 2, 2, D], BF16)
        gbc = sbuf([128, D], BF16)
        bbc = sbuf([128, D], BF16)
        pbc = sbuf([128, D], F32)
        b2bc = sbuf([128, D], F32)
        b1c = sbuf([128, ne, 32], F32)
        wec = sbuf([128, ne], F32)
        gatet = sbuf([128, 8, ne], F32)
        rot = sbuf([128, 128], BF16)
        onescb = sbuf([128, 1], BF16)
        onesrf = sbuf([1, 128], F32)
        mu2c = sbuf([128, NT], F32)
        rs2c = sbuf([128, NT], F32)
        qks = sbuf([128, 256], F32)
        rsc = sbuf([128, 4 * H], F32)
        cst = sbuf([128, 32], F32)
        seqc = sbuf([128, 8], F32)
        wcolb = sbuf([8, 1], F32)
        epsc = sbuf([128, 1], F32)
        PS = es.enter_context(nc.psum_tensor([128, 4096], F32))

        qA = WS[:, 0:4096].rearrange("p (t d) -> p t d", t=4)
        kA = WS[:, 4096:8192].rearrange("p (t d) -> p t d", t=4)
        vA = WS[:, 8192:12288].rearrange("p (t d) -> p t d", t=4)
        vA2 = WS[:, 12288:14336].rearrange("p (s d) -> p s d", s=2)
        Pb = WS[:, 14336:16384].rearrange("p (s d) -> p s d", s=2)
        PTb = WS[:, 16384:18432].rearrange("p (s d) -> p s d", s=2)
        w1r = WS[:, 0:8192].rearrange("p (s m k j) -> p s m k j", s=2, m=4, k=8)
        w2r = WS[:, 8192:16384].rearrange("p (s k d) -> p s k d", s=2, k=4)
        b2s = outb[0:8, 0, :]
        rowb = FP[0:1, 3, :]
        rsmall = FP[0:1, 2, 0:128]
        scr = FP[:, 0, :]
        scr2 = FP[:, 1, :]
        aos = FP[:, 2, :]
        ybuf = FP

        sems = {}
        for name in ["GX", "GP", "CQK", "CPJ",
                     "WQ0", "WQ1", "WP0", "WP1", "W10", "W11", "W20", "W21",
                     "TRS", "TRP", "TP0", "TP1", "ST0", "ST1",
                     "TEN", "SCE", "VEC", "PLE", "TQA", "TKA", "TVA"] + [f"CE{e}" for e in range(NE)]:
            sems[name] = es.enter_context(nc.semaphore(name))

        block = es.enter_context(nc.Block())
        P = Plan(strict_edges=strict_edges)

        def bank(b, width=512):
            return PS[:, 512 * b: 512 * b + width]

        def gq(sem, fn, waits=()):
            P.emit("gpsimd", fn, waits=waits, inc=sem, inc_by=16)
            return P.count(sem)

        c_R = gq("GX", lambda g: g.dma_start(R[:, :, :], dx.rearrange("(t p) d -> p t d", p=128)))
        gq("GP", lambda g: g.dma_start(b2s[:, :], db2))
        gq("GP", lambda g: g.dma_start(gbc[:, :], dgbc))
        gq("GP", lambda g: g.dma_start(bbc[:, :], dbbc))
        gq("GP", lambda g: g.dma_start(pbc[:, :], dpbc))
        gq("GP", lambda g: g.dma_start(rot[:, :], drot))
        gq("GP", lambda g: g.dma_start(onescb[:, :], dones))
        gq("GP", lambda g: g.dma_start(onesrf[:, :], donesr))
        gq("GP", lambda g: g.dma_start(b1c[:, :, :], db1.rearrange("e p m -> p e m")))
        c_params = gq("GP", lambda g: g.dma_start(gatet[:, :, :], dgate))
        c_wqkv = gq("CQK", lambda g: g.dma_start(swqkv, dwqkv))
        c_wproj = gq("CPJ", lambda g: g.dma_start(swproj, dwproj))
        c_wexp = []
        for e in range(ne):
            gq(f"CE{e}", lambda g, e=e: g.dma_start(sw1[e], dw1[e]))
            c_wexp.append(gq(f"CE{e}", lambda g, e=e: g.dma_start(sw2[e], dw2[e])))

        P.emit("vector", lambda v: v.memset(epsc[:, :], EPS),
               waits=[("GP", c_params)], inc="VEC")

        pslab_loaded = {}
        pslab_freed = {}

        def load_pslab(g):
            if g >= nch_a * 16:
                return
            s_ = g % 4
            w = [("CPJ", c_wproj)]
            if g >= 2:
                w.append(("TEN", pslab_freed[g - 2]))
            cnt = P.emit(
                "sync",
                lambda s, s_=s_, g=g: s.dma_start(wpr[:, g % 2, :, :],
                                                  swproj[:, 2 * s_:2 * s_ + 2, :]),
                waits=w, inc=f"WP{g % 2}", inc_by=16)
            pslab_loaded[g] = (f"WP{g % 2}", cnt)

        load_pslab(0)
        load_pslab(1)

        NSLAB_Q = 12
        qslab_loaded = {}
        qslab_freed = {}

        def load_qslab(g):
            c, s_ = divmod(g, NSLAB_Q)
            if c >= nch_a:
                return
            w = [("CQK", c_wqkv)]
            if g >= 2:
                w.append(("TEN", qslab_freed[g - 2]))
            cnt = P.emit(
                "sync",
                lambda s, s_=s_, g=g: s.dma_start(wqr[:, g % 2, :, :, :],
                                                  swqkv[:, 2 * s_:2 * s_ + 2, :, :]),
                waits=w, inc=f"WQ{g % 2}", inc_by=16)
            qslab_loaded[g] = (f"WQ{g % 2}", cnt)

        load_qslab(0)
        load_qslab(1)

        prev_xlnT = 0
        prev_avT = 0
        prev_qkv_end = 0
        prev_chunk_vec = 0
        prev_vrot_evac = None
        prev_ttsub = None
        prev_cexp = None
        av_done = {}
        av_ev = {}
        pt_cnt = {}

        chunk_xlnT = {}

        def emit_prologue(cc):
            nonlocal prev_xlnT
            t0p = 4 * cc
            vec_h = P.count("VEC")
            for tt in range(4):
                P.emit("vector",
                       lambda v, tt=tt, t0p=t0p: v.reduce_sum(cst[:, tt:tt + 1], R[:, t0p + tt, :], axis=AXX),
                       waits=[("GX", c_R)] if (cc == 0 and tt == 0) else [], inc="VEC")
            c_sum = P.count("VEC")
            for tt in range(4):
                wsq = [("SCE", P.count("SCE"))]
                if cc == 0 and tt == 0:
                    wsq.append(("GX", c_R))
                if tt == 0 and cc > 0:
                    wsq.append(("VEC", vec_h))
                P.emit("scalar",
                       lambda s, tt=tt, t0p=t0p: s.activation(scr[:, :], R[:, t0p + tt, :], AF.Square,
                                                              accum_out=cst[:, 4 + tt:5 + tt]),
                       waits=wsq, inc="SCE")
            c_sq = P.count("SCE")
            P.emit("vector", lambda v: v.tensor_scalar(cst[:, 8:12], cst[:, 0:4], 1.0 / D, None, op0=ALU.mult),
                   waits=[("VEC", c_sum)], inc="VEC")
            P.emit("vector", lambda v: v.tensor_tensor(cst[:, 12:16], cst[:, 8:12], cst[:, 8:12], op=ALU.mult),
                   inc="VEC")
            c_v1 = P.emit("vector",
                          lambda v: v.scalar_tensor_tensor(cst[:, 16:20], cst[:, 4:8], 1.0 / D,
                                                           cst[:, 12:16], op0=ALU.mult, op1=ALU.subtract),
                          waits=[("SCE", c_sq)], inc="VEC")
            c_ln1 = P.emit("scalar", lambda s: s.activation(cst[:, 20:24], cst[:, 16:20], AF.Ln, bias=epsc[:, 0:1]),
                           waits=[("VEC", c_v1)], inc="SCE")
            c_rs = P.emit("scalar", lambda s: s.activation(cst[:, 24:28], cst[:, 20:24], AF.Exp, scale=-0.5),
                          waits=[("SCE", c_ln1)], inc="SCE")
            ap_done = []
            for tt in range(4):
                w = [("SCE", c_rs)]
                if tt == 0 and cc > 0:
                    w += [("TRP", prev_xlnT), ("TRS", prev_avT)]
                cv = P.emit("vector",
                            lambda v, tt=tt, t0p=t0p: v.tensor_scalar(
                                xln[:, tt, :], R[:, t0p + tt, :],
                                cst[:, 8 + tt:9 + tt], cst[:, 24 + tt:25 + tt],
                                op0=ALU.subtract, op1=ALU.mult),
                            waits=w if tt == 0 else [], inc="VEC")
                ap_done.append(cv)
            for tt in range(4):
                w = [("VEC", ap_done[tt])]
                if tt == 0 and cc > 0:
                    w.append(("TEN", prev_qkv_end))
                P.emit("sync",
                       lambda s, tt=tt: s.dma_start(
                           xlnT[:, :, tt * 128:tt * 128 + 128],
                           xln[:, tt, :].rearrange("p (k j) -> p k j", j=128),
                           transpose=True),
                       waits=w, inc="TRP", inc_by=16)
            chunk_xlnT[cc] = P.count("TRP")
            prev_xlnT = chunk_xlnT[cc]

        emit_prologue(0)
        for c in range(nch_a):
            t0 = 4 * c
            c_xlnT = chunk_xlnT[c]

            evac_cnt = {}
            grp_T = {}
            for m in range(24):
                g = c * NSLAB_Q + m // 2
                w = [qslab_loaded[g], ("TRP", c_xlnT)]
                if m >= 2:
                    w.append(evac_cnt[m - 2])
                for k in range(8):
                    cm = P.emit("tensor",
                                lambda t, m=m, k=k, g=g: t.matmul(
                                    bank(m % 2),
                                    wqr[:, g % 2, m % 2, k, :],
                                    xlnT[:, k, :],
                                    start=(k == 0), stop=(k == 7)),
                                waits=w if k == 0 else [], inc="TEN")
                if m % 2 == 1:
                    qslab_freed[g] = cm
                    load_qslab(g + 2)
                ce = P.emit("scalar",
                            lambda s, m=m: s.activation(qkvB[:, m, :], bank(m % 2), AF.Copy),
                            waits=[("TEN", cm)], inc="SCE")
                evac_cnt[m] = ("SCE", ce)
                dst = (qA, kA, vA)[m // 8]
                gsem = ("TQA", "TKA", "TVA")[m // 8]
                P.emit("sync",
                       lambda s, m=m, dst=dst: s.dma_start(
                           dst[:, :, (m % 8) * 128:(m % 8) * 128 + 128],
                           qkvB[:, m, :].rearrange("p (t j) -> p t j", j=128),
                           transpose=True),
                       waits=[evac_cnt[m]], inc=gsem, inc_by=16)
                if m % 8 == 7:
                    grp_T[m // 8] = (gsem, P.count(gsem))
            prev_qkv_end = cm
            c_vA = grp_T[2]

            red_of = {}
            for si in range(2):
                src = (qA, kA)[si]
                for tt in range(4):
                    o = si * 128 + tt * 16
                    P.emit("vector",
                           lambda v, src=src, tt=tt, o=o: v.tensor_reduce(
                               qks[:, o:o + 16].rearrange("p (h x) -> p h x", x=1),
                               src[:, tt, :].rearrange("p (h x) -> p h x", x=64),
                               axis=AXX, op=ALU.add),
                           waits=[grp_T[si]] if tt == 0 else [], inc="VEC")
            prev_red = None
            for si in range(2):
                src = (qA, kA)[si]
                for tt in range(4):
                    i8 = si * 4 + tt
                    o = si * 128 + tt * 16
                    sq_dst = scr if i8 % 2 == 0 else scr2
                    wq = [grp_T[si]] if tt == 0 else []
                    wq.append(("SCE", P.count("SCE")))
                    if i8 == 0 and c > 0:
                        wq.append(("VEC", prev_chunk_vec))
                    if i8 >= 2:
                        wq.append(("VEC", red_of[i8 - 2]))
                    cq = P.emit("scalar",
                                lambda s, src=src, tt=tt, sq_dst=sq_dst: s.activation(
                                    sq_dst[:, :], src[:, tt, :], AF.Square),
                                waits=wq, inc="SCE")
                    prev_red = P.emit("vector",
                                      lambda v, o=o, sq_dst=sq_dst: v.tensor_reduce(
                                          qks[:, 64 + o:64 + o + 16].rearrange("p (h x) -> p h x", x=1),
                                          sq_dst[:, :].rearrange("p (h x) -> p h x", x=64),
                                          axis=AXX, op=ALU.add),
                                      waits=[("SCE", cq)], inc="VEC")
                    red_of[i8] = prev_red
            c_qs = prev_red
            P.emit("vector", lambda v: v.tensor_scalar(qks[:, 0:64], qks[:, 0:64], 1.0 / HD, None, op0=ALU.mult),
                   waits=[("VEC", c_qs)], inc="VEC")
            P.emit("vector", lambda v: v.tensor_scalar(qks[:, 128:192], qks[:, 128:192], 1.0 / HD, None, op0=ALU.mult),
                   inc="VEC")
            P.emit("vector", lambda v: v.tensor_tensor(scr2[:, 0:64], qks[:, 0:64], qks[:, 0:64], op=ALU.mult),
                   inc="VEC")
            P.emit("vector", lambda v: v.tensor_tensor(scr2[:, 64:128], qks[:, 128:192], qks[:, 128:192], op=ALU.mult),
                   inc="VEC")
            P.emit("vector",
                   lambda v: v.scalar_tensor_tensor(qks[:, 64:128], qks[:, 64:128], 1.0 / HD,
                                                    scr2[:, 0:64], op0=ALU.mult, op1=ALU.subtract),
                   inc="VEC")
            cvv = P.emit("vector",
                         lambda v: v.scalar_tensor_tensor(qks[:, 192:256], qks[:, 192:256], 1.0 / HD,
                                                          scr2[:, 64:128], op0=ALU.mult, op1=ALU.subtract),
                         inc="VEC")
            cl1 = P.emit("scalar", lambda s: s.activation(scr2[:, 0:64],
                                                          qks[:, 64:128], AF.Ln, bias=epsc[:, 0:1]),
                         waits=[("VEC", cvv)], inc="SCE")
            cl2 = P.emit("scalar", lambda s: s.activation(qks[:, 64:128], scr2[:, 0:64], AF.Exp, scale=-0.5),
                         waits=[("SCE", cl1)], inc="SCE")
            cl3 = P.emit("scalar", lambda s: s.activation(scr2[:, 128:192], qks[:, 192:256], AF.Ln, bias=epsc[:, 0:1]),
                         inc="SCE")
            c_qkrs = P.emit("scalar", lambda s: s.activation(qks[:, 192:256], scr2[:, 128:192], AF.Exp, scale=-0.5),
                            waits=[("SCE", cl3)], inc="SCE")

            qk_ap = {}
            for si in range(2):
                src = (qA, kA)[si]
                for tp_ in range(2):
                    on_pool = (tp_ == 1)
                    eng = "gpsimd" if on_pool else "vector"
                    esem = "PLE" if on_pool else "VEC"
                    sscr = aos if on_pool else scr
                    om = si * 128 + tp_ * 32
                    ors = om + 64
                    w0 = [("SCE", c_qkrs if si == 1 else cl2)]
                    P.emit(eng,
                           lambda v, src=src, tp_=tp_, om=om, sscr=sscr: v.tensor_tensor(
                               sscr[:, :].rearrange("p (h j) -> p h j", j=64),
                               src[:, 2 * tp_, :].rearrange("p (h j) -> p h j", j=64),
                               qks[:, om:om + 16].unsqueeze(-1).broadcast_to([128, 16, 64]),
                               op=ALU.subtract),
                           waits=w0 if (tp_ == 0 or on_pool) else [], inc=esem)
                    P.emit(eng,
                           lambda v, ors=ors, sscr=sscr: v.tensor_tensor(
                               sscr[:, :].rearrange("p (h j) -> p h j", j=64),
                               sscr[:, :].rearrange("p (h j) -> p h j", j=64),
                               qks[:, ors:ors + 16].unsqueeze(-1).broadcast_to([128, 16, 64]),
                               op=ALU.mult),
                           inc=esem)
                    P.emit(eng,
                           lambda v, sscr=sscr: v.tensor_tensor(sscr[:, :], sscr[:, :], gbc[:, :], op=ALU.mult),
                           inc=esem)
                    ca_t0 = P.emit(eng,
                                   lambda v, src=src, tp_=tp_, sscr=sscr: v.tensor_tensor(
                                       src[:, 2 * tp_, :], sscr[:, :], bbc[:, :], op=ALU.add),
                                   inc=esem)
                    qk_ap[(si, 2 * tp_)] = (esem, ca_t0)
                    om2 = om + 16
                    ors2 = om2 + 64
                    P.emit(eng,
                           lambda v, src=src, tp_=tp_, om2=om2, sscr=sscr: v.tensor_tensor(
                               sscr[:, :].rearrange("p (h j) -> p h j", j=64),
                               src[:, 2 * tp_ + 1, :].rearrange("p (h j) -> p h j", j=64),
                               qks[:, om2:om2 + 16].unsqueeze(-1).broadcast_to([128, 16, 64]),
                               op=ALU.subtract),
                           inc=esem)
                    P.emit(eng,
                           lambda v, ors2=ors2, sscr=sscr: v.tensor_tensor(
                               sscr[:, :].rearrange("p (h j) -> p h j", j=64),
                               sscr[:, :].rearrange("p (h j) -> p h j", j=64),
                               qks[:, ors2:ors2 + 16].unsqueeze(-1).broadcast_to([128, 16, 64]),
                               op=ALU.mult),
                           inc=esem)
                    P.emit(eng,
                           lambda v, sscr=sscr: v.tensor_tensor(sscr[:, :], sscr[:, :], gbc[:, :], op=ALU.mult),
                           inc=esem)
                    ca = P.emit(eng,
                                lambda v, src=src, tp_=tp_, sscr=sscr: v.tensor_tensor(
                                    src[:, 2 * tp_ + 1, :], sscr[:, :], bbc[:, :], op=ALU.add),
                                inc=esem)
                    qk_ap[(si, 2 * tp_ + 1)] = (esem, ca)

            for si in range(2):
                srcb = (qA, kA)[si]
                for tt in range(4):
                    P.emit("sync",
                           lambda s, si=si, srcb=srcb, tt=tt: s.dma_start(
                               qkvB[:, si * 8:si * 8 + 8, tt * 128:tt * 128 + 128],
                               srcb[:, tt, :].rearrange("p (k j) -> p k j", j=128),
                               transpose=True),
                           waits=[qk_ap[(si, tt)]],
                           inc="TRS", inc_by=16)
            c_qkT = P.count("TRS")

            score4d_in = PS[:, 1024:3072].rearrange("p (b x) -> p b x", b=4)[:, :, 0:256] \
                .rearrange("p b (g j) -> p b g j", j=64)
            for wp in range(4):
                gwp = 4 * c + wp
                slot = gwp % 2
                w0 = [("TRS", c_qkT)]
                if wp == 0 and c > 0:
                    w0.append(("VEC", prev_chunk_vec))
                if prev_ttsub is not None:
                    w0.append(("VEC", prev_ttsub))
                if prev_vrot_evac is not None:
                    w0.append(prev_vrot_evac)
                first = True
                for hh in range(H):
                    mq = hh // 2
                    pbase = 64 * (hh % 2)
                    colq = 1024 + 512 * (hh % 4) + 64 * (hh // 4)
                    for wpar in range(2):
                        cm = P.emit("tensor",
                                    lambda t, mq=mq, pbase=pbase, colq=colq, wpar=wpar, wp=wp: t.matmul(
                                        PS[64 * wpar:64 * wpar + 64, colq:colq + 64],
                                        qkvB[pbase:pbase + 64, mq, wp * 128 + 64 * wpar: wp * 128 + 64 * wpar + 64],
                                        qkvB[pbase:pbase + 64, 8 + mq, wp * 128 + 64 * wpar: wp * 128 + 64 * wpar + 64],
                                        start=True, stop=True,
                                        tile_position=(pbase, 64 * wpar)),
                                    waits=w0 if first else [], inc="TEN")
                        first = False
                c_sc = cm
                cmx = P.emit("vector",
                             lambda v: v.tensor_reduce(
                                 rsc[:, 0:16].rearrange("p (b g x) -> p b g x", g=4, x=1),
                                 score4d_in, axis=AXX, op=ALU.max, negate=True),
                             waits=[("TEN", c_sc)], inc="VEC")
                wsub = [("VEC", cmx)]
                if prev_cexp is not None:
                    wsub.append(("SCE", prev_cexp))
                csub = P.emit("vector",
                              lambda v: v.tensor_tensor(
                                  scr2[:, :].rearrange("p (b g j) -> p b g j", b=4, j=64),
                                  score4d_in,
                                  rsc[:, 0:16].rearrange("p (b g) -> p b g", g=4).unsqueeze(-1)
                                  .broadcast_to([128, 4, 4, 64]),
                                  op=ALU.add),
                              waits=wsub, inc="VEC")
                prev_ttsub = csub
                wexp = [("VEC", csub)]
                if gwp >= 2:
                    wexp.append(pt_cnt[gwp - 2])
                cexp = P.emit("scalar",
                              lambda s, slot=slot: s.activation(Pb[:, slot, :], scr2[:, :], AF.Exp, scale=SCALE),
                              waits=wexp, inc="SCE")
                prev_cexp = cexp
                csm = P.emit("vector",
                             lambda v, slot=slot: v.tensor_reduce(
                                 rsc[:, 16 + 16 * slot:32 + 16 * slot].rearrange("p (h x) -> p h x", x=1),
                                 Pb[:, slot, :].rearrange("p (h j) -> p h j", j=64),
                                 axis=AXX, op=ALU.add),
                             waits=[("SCE", cexp)], inc="VEC")
                crc = P.emit("vector",
                             lambda v, slot=slot: v.reciprocal(rsc[:, 16 + 16 * slot:32 + 16 * slot],
                                                               rsc[:, 16 + 16 * slot:32 + 16 * slot]),
                             waits=[("VEC", csm)], inc="VEC")
                w1_ = [("VEC", csub), c_vA]
                if prev_vrot_evac is not None:
                    w1_.append(prev_vrot_evac)
                if gwp >= 2:
                    w1_.append(("TEN", av_done[gwp - 2]))
                P.emit("tensor",
                       lambda t, wp=wp: t.matmul(bank(2), rot[:, :], vA[:, wp, 0:512],
                                                 start=True, stop=True),
                       waits=w1_, inc="TEN")
                cr2 = P.emit("tensor",
                             lambda t, wp=wp: t.matmul(bank(3), rot[:, :], vA[:, wp, 512:1024],
                                                       start=True, stop=True),
                             inc="TEN")
                cve = P.emit("scalar",
                             lambda s, slot=slot: s.activation(vA2[:, slot, :], PS[:, 1024:2048], AF.Copy),
                             waits=[("TEN", cr2)], inc="SCE")
                prev_vrot_evac = ("SCE", cve)
                wt = [("SCE", cexp)]
                if gwp >= 2:
                    wt.append(("TEN", av_done[gwp - 2]))
                P.emit("scalar",
                       lambda s, slot=slot: s.dma_start(
                           PTb[:, slot, :].rearrange("p (a j) -> p a j", j=128),
                           Pb[:, slot, :].rearrange("p (a j) -> p a j", j=128),
                           transpose=True),
                       waits=wt, inc=f"TP{gwp % 2}", inc_by=16)
                pt_cnt[gwp] = (f"TP{gwp % 2}", P.count(f"TP{gwp % 2}"))
                avw = [pt_cnt[gwp], ("SCE", cve), c_vA]
                if gwp >= 2:
                    avw.append(av_ev[gwp - 2])
                first = True
                for s_ in range(H):
                    pbank = (0, 1, 6, 7)[s_ % 4]
                    colv = 512 * pbank + 64 * (s_ // 4)
                    ktbase = 64 * (s_ % 2)
                    jt = s_ // 2
                    h_true = 4 * (s_ % 4) + s_ // 4
                    for wpar in range(2):
                        vsel = ((s_ % 2) == wpar)
                        cm = P.emit("tensor",
                                    lambda t, wp=wp, slot=slot, jt=jt, ktbase=ktbase, wpar=wpar,
                                    h_true=h_true, colv=colv, vsel=vsel: t.matmul(
                                        PS[64 * wpar:64 * wpar + 64, colv:colv + 64],
                                        PTb[ktbase:ktbase + 64, slot, jt * 128 + 64 * wpar: jt * 128 + 64 * wpar + 64],
                                        (vA[ktbase:ktbase + 64, wp, h_true * 64:h_true * 64 + 64] if vsel
                                         else vA2[ktbase:ktbase + 64, slot, h_true * 64:h_true * 64 + 64]),
                                        start=True, stop=True,
                                        tile_position=(ktbase, 64 * wpar)),
                                    waits=avw if first else [], inc="TEN")
                        first = False
                av_done[gwp] = cm
                last = None
                for gsel in range(2):
                    boff = 0 if gsel == 0 else 3072
                    last = P.emit("vector",
                                  lambda v, wp=wp, slot=slot, boff=boff, gsel=gsel: v.tensor_tensor(
                                      xln[:, wp, :].rearrange("p (x y j) -> p x y j", x=4, j=64)
                                      [:, 2 * gsel:2 * gsel + 2, :, :],
                                      PS[:, boff:boff + 1024].rearrange("p (b x) -> p b x", b=2)[:, :, 0:256]
                                      .rearrange("p b (g j) -> p b g j", j=64),
                                      rsc[:, 16 + 16 * slot:32 + 16 * slot]
                                      .rearrange("p (y x) -> p x y", x=4)[:, 2 * gsel:2 * gsel + 2, :]
                                      .unsqueeze(-1).broadcast_to([128, 2, 4, 64]),
                                      op=ALU.mult),
                                  waits=[("TEN", av_done[gwp]), ("VEC", crc)], inc="VEC")
                av_ev[gwp] = ("VEC", last)

            for tt in range(4):
                P.emit("sync",
                       lambda s, tt=tt: s.dma_start(
                           qkvB[:, 16:24, tt * 128:tt * 128 + 128],
                           xln[:, tt, :].rearrange("p (k j) -> p k j", j=128),
                           transpose=True),
                       waits=[av_ev[4 * c + tt]], inc="TRS", inc_by=16)
            c_avT = P.count("TRS")
            prev_avT = c_avT

            if c + 1 < nch_a:
                emit_prologue(c + 1)

            epi = {}
            for tt in range(4):
                pb0 = 2 + 2 * (tt % 2)
                w0 = [("TRS", c_avT)]
                if tt == 0:
                    w0.append(("VEC", prev_ttsub))
                    w0.append(prev_vrot_evac)
                if tt >= 2:
                    w0.append(epi[tt - 2])
                first = True
                for k in range(8):
                    gp = (c * 4 + tt) * 4 + k // 2
                    wk = [pslab_loaded[gp]]
                    for half in range(2):
                        wts = (w0 + wk) if first else (wk if (half == 0 and k % 2 == 0) else [])
                        cm = P.emit("tensor",
                                    lambda t, tt=tt, k=k, half=half, pb0=pb0, gp=gp: t.matmul(
                                        bank(pb0 + half),
                                        qkvB[:, 16 + k, tt * 128:tt * 128 + 128],
                                        wpr[:, gp % 2, k % 2, 512 * half:512 * half + 512],
                                        start=(k == 0), stop=(k == 7)),
                                    waits=wts, inc="TEN")
                        first = False
                    if k % 2 == 1:
                        pslab_freed[gp] = cm
                        load_pslab(gp + 2)
                ce1 = P.emit("vector",
                             lambda v, pb0=pb0: v.tensor_tensor(aos[:, :], PS[:, 512 * pb0:512 * pb0 + 1024],
                                                                pbc[:, :], op=ALU.add),
                             waits=[("TEN", cm), ("GP", c_params)], inc="VEC")
                cs1 = P.emit("vector", lambda v: v.reduce_sum(cst[:, 28:29], aos[:, :], axis=AXX),
                             waits=[("VEC", ce1)], inc="VEC")
                cq1 = P.emit("scalar", lambda s: s.activation(scr[:, :], aos[:, :], AF.Square,
                                                              accum_out=cst[:, 29:30]),
                             waits=[("VEC", ce1), ("SCE", P.count("SCE"))], inc="SCE")
                cm1 = P.emit("vector", lambda v: v.tensor_scalar(cst[:, 28:29], cst[:, 28:29], 1.0 / D, None, op0=ALU.mult),
                             waits=[("VEC", cs1)], inc="VEC")
                cm2 = P.emit("vector", lambda v: v.tensor_tensor(cst[:, 30:31], cst[:, 28:29], cst[:, 28:29], op=ALU.mult),
                             waits=[("VEC", cm1)], inc="VEC")
                cm3 = P.emit("vector",
                             lambda v: v.scalar_tensor_tensor(cst[:, 30:31], cst[:, 29:30], 1.0 / D,
                                                              cst[:, 30:31], op0=ALU.mult, op1=ALU.subtract),
                             waits=[("SCE", cq1), ("VEC", cm2)], inc="VEC")
                cln = P.emit("scalar", lambda s: s.activation(cst[:, 31:32], cst[:, 30:31], AF.Ln, bias=epsc[:, 0:1]),
                             waits=[("VEC", cm3)], inc="SCE")
                crs = P.emit("scalar", lambda s: s.activation(cst[:, 30:31], cst[:, 31:32], AF.Exp, scale=-0.5),
                             waits=[("SCE", cln)], inc="SCE")
                ca1 = P.emit("vector",
                             lambda v: v.tensor_scalar(scr2[:, :], aos[:, :], cst[:, 28:29], cst[:, 30:31],
                                                       op0=ALU.subtract, op1=ALU.mult),
                             waits=[("SCE", crs)], inc="VEC")
                ca2 = P.emit("vector",
                             lambda v, tt=tt, t0=t0: v.tensor_tensor(R[:, t0 + tt, :], scr2[:, :],
                                                                     R[:, t0 + tt, :], op=ALU.add),
                             inc="VEC")
                epi[tt] = ("VEC", ca2)
            prev_chunk_vec = P.count("VEC")

        if attn_only:
            npair = 0
        c_epi_all = P.count("VEC")
        prev_sq = None
        for t_ in (range(NT) if npair else []):
            P.emit("vector", lambda v, t_=t_: v.reduce_sum(mu2c[:, t_:t_ + 1], R[:, t_, :], axis=AXX),
                   waits=[("VEC", c_epi_all)] if t_ == 0 else [], inc="VEC")
            prev_sq = P.emit("scalar", lambda s, t_=t_: s.activation(scr[:, :], R[:, t_, :], AF.Square,
                                                                    accum_out=rs2c[:, t_:t_ + 1]),
                             waits=([("VEC", c_epi_all)] if t_ == 0 else []) + [("SCE", P.count("SCE"))],
                             inc="SCE")
        c_s1 = P.count("VEC")
        c_s2 = P.count("SCE")
        if npair == 0:
            for t_ in range(NT):
                cflush = P.emit("vector",
                                lambda v, t_=t_: v.tensor_copy(outb[:, t_ % 2, :], R[:, t_, :]),
                                waits=[("VEC", c_epi_all)] + ([(f"ST{t_ % 2}", 16 * ((t_ - 2) // 2 + 1))] if t_ >= 2 else []),
                                inc="VEC")
                P.emit("gpsimd",
                       lambda g, t_=t_: g.dma_start(dout[t_ * 128:(t_ + 1) * 128, :], outb[:, t_ % 2, :]),
                       waits=[("VEC", cflush)], inc=f"ST{t_ % 2}", inc_by=16)
            P.emit("gpsimd", lambda g: None, waits=[("ST0", P.count("ST0")), ("ST1", P.count("ST1"))])
            P.emit("sync", lambda s: None, waits=[("ST0", P.count("ST0")), ("ST1", P.count("ST1"))])
        else:
            _dummy = 0
        if npair:
            P.emit("vector", lambda v: v.tensor_scalar(mu2c[:, :], mu2c[:, :], 1.0 / D, None, op0=ALU.mult),
                   waits=[("VEC", c_s1)], inc="VEC")
            P.emit("vector", lambda v: v.tensor_tensor(cst[:, 0:NT], mu2c[:, :], mu2c[:, :], op=ALU.mult),
                   inc="VEC")
            cv2 = P.emit("vector",
                         lambda v: v.scalar_tensor_tensor(rs2c[:, :], rs2c[:, :], 1.0 / D, cst[:, 0:NT],
                                                          op0=ALU.mult, op1=ALU.subtract),
                         waits=[("SCE", c_s2)], inc="VEC")
            cln2 = P.emit("scalar", lambda s: s.activation(rs2c[:, :], rs2c[:, :], AF.Ln, bias=epsc[:, 0:1]),
                          waits=[("VEC", cv2)], inc="SCE")
            c_rs2 = P.emit("scalar", lambda s: s.activation(rs2c[:, :], rs2c[:, :], AF.Exp, scale=-0.5),
                           waits=[("SCE", cln2)], inc="SCE")
            prev_mm = None
            for t_ in range(NT):
                w = [("SCE", c_rs2)]
                if prev_mm is not None and t_ >= 2:
                    w.append(("TEN", prev_mm))
                cap = P.emit("vector",
                             lambda v, t_=t_: v.tensor_scalar(xln[:, t_ % 2, :], R[:, t_, :],
                                                              mu2c[:, t_:t_ + 1], rs2c[:, t_:t_ + 1],
                                                              op0=ALU.subtract, op1=ALU.mult),
                             waits=w, inc="VEC")
                for half in range(2):
                    prev_mm = P.emit("tensor",
                                     lambda t, t_=t_, half=half: t.matmul(
                                         PS[0:1, 512 * half:512 * half + 512],
                                         onescb[:, :],
                                         xln[:, t_ % 2, 512 * half:512 * half + 512],
                                         start=(t_ == 0), stop=(t_ == NT - 1)),
                                     waits=[("VEC", cap)] if half == 0 else [], inc="TEN")
            c_seq = prev_mm
            cr0 = P.emit("vector", lambda v: v.tensor_copy(rowb[0:1, 0:1024], PS[0:1, 0:1024]),
                         waits=[("TEN", c_seq)], inc="VEC")
            for k in range(8):
                cm = P.emit("tensor",
                            lambda t, k=k: t.matmul(PS[:, 512 * k + 256:512 * k + 257],
                                                    rowb[0:1, 128 * k:128 * k + 128],
                                                    onesrf[0:1, 0:1], start=True, stop=True),
                            waits=[("VEC", cr0)] if k == 0 else [], inc="TEN")
            c_sqc = cm
            cg = P.emit("vector",
                        lambda v: v.tensor_scalar(
                            seqc[:, :],
                            PS[:, :].rearrange("p (k x) -> p k x", x=512)[:, :, 256:257]
                            .rearrange("p k x -> p (k x)"),
                            1.0 / ntok, None, op0=ALU.mult),
                        waits=[("TEN", c_sqc)], inc="VEC")
            for k in range(8):
                cm = P.emit("tensor",
                            lambda t, k=k: t.matmul(PS[0:1, 0:ne], seqc[:, k:k + 1], gatet[:, k, :],
                                                    start=(k == 0), stop=(k == 7)),
                            waits=[("VEC", cg)] if k == 0 else [], inc="TEN")
            c_gate = cm
            LO, NM, EX, SM, PR, M1, PM, M2, KP, WR = 0, 16, 24, 32, 40, 48, 56, 64, 72, 80
            r_ = rsmall
            c1 = P.emit("vector", lambda v: v.tensor_copy(r_[0:1, LO:LO + ne], PS[0:1, 0:ne]),
                        waits=[("TEN", c_gate)], inc="VEC")
            c2 = P.emit("vector", lambda v: v.tensor_reduce(r_[0:1, NM:NM + 1], r_[0:1, LO:LO + ne],
                                                            axis=AXX, op=ALU.max, negate=True),
                        waits=[("VEC", c1)], inc="VEC")
            c2b = P.emit("scalar", lambda s: s.activation(r_[0:1, NM + 1:NM + 2], r_[0:1, NM:NM + 1], AF.Copy),
                         waits=[("VEC", c2)], inc="SCE")
            c3 = P.emit("vector", lambda v: v.tensor_scalar(r_[0:1, EX:EX + ne], r_[0:1, LO:LO + ne],
                                                            r_[0:1, NM + 1:NM + 2], None, op0=ALU.add),
                        waits=[("SCE", c2b)], inc="VEC")
            c4 = P.emit("scalar", lambda s: s.activation(r_[0:1, SM:SM + ne], r_[0:1, EX:EX + ne], AF.Exp),
                        waits=[("VEC", c3)], inc="SCE")
            c5 = P.emit("vector", lambda v: v.tensor_reduce(r_[0:1, M1:M1 + 1], r_[0:1, SM:SM + ne], axis=AXX, op=ALU.add),
                        waits=[("SCE", c4)], inc="VEC")
            c5b = P.emit("scalar", lambda s: s.activation(r_[0:1, M1 + 2:M1 + 3], r_[0:1, M1:M1 + 1], AF.Copy),
                         waits=[("VEC", c5)], inc="SCE")
            c6 = P.emit("vector", lambda v: v.reciprocal(r_[0:1, M1 + 1:M1 + 2], r_[0:1, M1 + 2:M1 + 3]),
                        waits=[("SCE", c5b)], inc="VEC")
            c6b = P.emit("scalar", lambda s: s.activation(r_[0:1, M1 + 3:M1 + 4], r_[0:1, M1 + 1:M1 + 2], AF.Copy),
                         waits=[("VEC", c6)], inc="SCE")
            c7 = P.emit("vector", lambda v: v.tensor_scalar(r_[0:1, PR:PR + ne], r_[0:1, SM:SM + ne],
                                                            r_[0:1, M1 + 3:M1 + 4], None, op0=ALU.mult),
                        waits=[("SCE", c6b)], inc="VEC")
            c8 = P.emit("vector", lambda v: v.tensor_reduce(r_[0:1, M1:M1 + 1], r_[0:1, PR:PR + ne], axis=AXX, op=ALU.max),
                        waits=[("VEC", c7)], inc="VEC")
            c8b = P.emit("scalar", lambda s: s.activation(r_[0:1, M1 + 2:M1 + 3], r_[0:1, M1:M1 + 1], AF.Copy),
                         waits=[("VEC", c8)], inc="SCE")
            c9 = P.emit("vector", lambda v: v.tensor_scalar(r_[0:1, KP:KP + ne], r_[0:1, PR:PR + ne],
                                                            r_[0:1, M1 + 2:M1 + 3], None, op0=ALU.is_ge),
                        waits=[("SCE", c8b)], inc="VEC")
            c10 = P.emit("vector", lambda v: v.scalar_tensor_tensor(r_[0:1, PM:PM + ne], r_[0:1, KP:KP + ne], -1e30,
                                                                    r_[0:1, PR:PR + ne], op0=ALU.mult, op1=ALU.add),
                         waits=[("VEC", c9)], inc="VEC")
            c11 = P.emit("vector", lambda v: v.tensor_reduce(r_[0:1, M2:M2 + 1], r_[0:1, PM:PM + ne], axis=AXX, op=ALU.max),
                         waits=[("VEC", c10)], inc="VEC")
            c11b = P.emit("scalar", lambda s: s.activation(r_[0:1, M2 + 1:M2 + 2], r_[0:1, M2:M2 + 1], AF.Copy),
                          waits=[("VEC", c11)], inc="SCE")
            c12 = P.emit("vector", lambda v: v.tensor_scalar(r_[0:1, KP:KP + ne], r_[0:1, PR:PR + ne],
                                                             r_[0:1, M2 + 1:M2 + 2], None, op0=ALU.is_ge),
                         waits=[("SCE", c11b)], inc="VEC")
            c13 = P.emit("vector", lambda v: v.tensor_tensor(r_[0:1, WR:WR + ne], r_[0:1, PR:PR + ne],
                                                             r_[0:1, KP:KP + ne], op=ALU.mult),
                         waits=[("VEC", c12)], inc="VEC")
            c14 = P.emit("vector", lambda v: v.tensor_reduce(r_[0:1, M2:M2 + 1], r_[0:1, WR:WR + ne], axis=AXX, op=ALU.add),
                         waits=[("VEC", c13)], inc="VEC")
            c14b = P.emit("scalar", lambda s: s.activation(r_[0:1, M2 + 2:M2 + 3], r_[0:1, M2:M2 + 1], AF.Copy),
                          waits=[("VEC", c14)], inc="SCE")
            c15 = P.emit("vector", lambda v: v.reciprocal(r_[0:1, M2 + 1:M2 + 2], r_[0:1, M2 + 2:M2 + 3]),
                         waits=[("SCE", c14b)], inc="VEC")
            c15b = P.emit("scalar", lambda s: s.activation(r_[0:1, M2 + 3:M2 + 4], r_[0:1, M2 + 1:M2 + 2], AF.Copy),
                          waits=[("VEC", c15)], inc="SCE")
            c16 = P.emit("vector", lambda v: v.tensor_scalar(r_[0:1, WR:WR + ne], r_[0:1, WR:WR + ne],
                                                             r_[0:1, M2 + 3:M2 + 4], None, op0=ALU.mult),
                         waits=[("SCE", c15b)], inc="VEC")
            cwb = P.emit("tensor", lambda t: t.matmul(PS[:, 512:512 + ne], onesrf[:, :], r_[0:1, WR:WR + ne],
                                                      start=True, stop=True),
                         waits=[("VEC", c16)], inc="TEN")
            cwe = P.emit("vector", lambda v: v.tensor_copy(wec[:, :], PS[:, 512:512 + ne]),
                         waits=[("TEN", cwb)], inc="VEC")
            cwc = P.emit("tensor", lambda t: t.matmul(PS[0:8, 1024:1025], r_[0:1, WR:WR + ne], onesrf[0:1, 0:1],
                                                      start=True, stop=True),
                         waits=[("VEC", c16)], inc="TEN")
            cwc2 = P.emit("vector", lambda v: v.tensor_copy(wcolb[:, :], PS[0:8, 1024:1025]),
                          waits=[("TEN", cwc)], inc="VEC")
            cbe = None
            for half in range(2):
                cbe = P.emit("tensor",
                             lambda t, half=half: t.matmul(PS[0:1, 1536 + 512 * half:2048 + 512 * half],
                                                           wcolb[:, :], b2s[:, 512 * half:512 * half + 512],
                                                           start=True, stop=True),
                             waits=[("VEC", cwc2), ("GP", c_params)], inc="TEN")
            cbr = P.emit("vector", lambda v: v.tensor_copy(rowb[0:1, 0:1024], PS[0:1, 1536:2560]),
                         waits=[("TEN", cbe)], inc="VEC")
            cbb = None
            for half in range(2):
                cbb = P.emit("tensor",
                             lambda t, half=half: t.matmul(PS[:, 2560 + 512 * half:3072 + 512 * half],
                                                           onesrf[:, :], rowb[0:1, 512 * half:512 * half + 512],
                                                           start=True, stop=True),
                             waits=[("VEC", cbr)], inc="TEN")
            c_b2bc = P.emit("vector", lambda v: v.tensor_copy(b2bc[:, :], PS[:, 2560:3584]),
                            waits=[("TEN", cbb)], inc="VEC")

        NS1 = 8
        NS2 = 8
        w1slab_loaded = {}
        w1slab_freed = {}
        w2slab_loaded = {}
        w2slab_freed = {}
        c_ws_start = P.count("TEN")

        def load_w1slab(gs):
            pe, s_ = divmod(gs, NS1)
            p_, e = divmod(pe, ne)
            if p_ >= npair:
                return
            w = [(f"CE{e}", c_wexp[e])]
            if gs >= 2:
                w.append(("TEN", w1slab_freed[gs - 2]))
            else:
                w.append(("TEN", c_ws_start))
            cnt = P.emit(
                "sync",
                lambda s, e=e, s_=s_, gs=gs: s.dma_start(
                    w1r[:, gs % 2, :, :, :], sw1[e][:, 4 * s_:4 * s_ + 4, :, :]),
                waits=w, inc=f"W1{gs % 2}", inc_by=16)
            w1slab_loaded[gs] = (f"W1{gs % 2}", cnt)

        def load_w2slab(gs):
            pe, s_ = divmod(gs, NS2)
            p_, e = divmod(pe, ne)
            if p_ >= npair:
                return
            w = [(f"CE{e}", c_wexp[e])]
            if gs >= 2:
                w.append(("TEN", w2slab_freed[gs - 2]))
            else:
                w.append(("TEN", c_ws_start))
            cnt = P.emit(
                "sync",
                lambda s, e=e, s_=s_, gs=gs: s.dma_start(
                    w2r[:, gs % 2, :, :], sw2[e][:, 4 * s_:4 * s_ + 4, :]),
                waits=w, inc=f"W2{gs % 2}", inc_by=16)
            w2slab_loaded[gs] = (f"W2{gs % 2}", cnt)

        load_w1slab(0)
        load_w1slab(1)
        load_w2slab(0)
        load_w2slab(1)

        yev = {}
        gelu_cnt = {}
        prev_pair_ten = None
        prev_xT = 0
        c_mm1_last = {}
        pair_xT = {}

        def emit_pair_prefix(p_):
            nonlocal prev_xT
            t0 = 4 * p_
            apd = []
            for tt in range(4):
                w = []
                if tt == 0:
                    w = [("VEC", c_b2bc)]
                    if p_ > 0:
                        w.append(("TEN", c_mm1_last[p_ - 1]))
                    if prev_xT:
                        w.append(("TRP", prev_xT))
                cap = P.emit("vector",
                             lambda v, tt=tt, t0=t0: v.tensor_scalar(
                                 xln[:, tt, :], R[:, t0 + tt, :],
                                 mu2c[:, t0 + tt:t0 + tt + 1], rs2c[:, t0 + tt:t0 + tt + 1],
                                 op0=ALU.subtract, op1=ALU.mult),
                             waits=w, inc="VEC")
                apd.append(cap)
            for tt in range(4):
                P.emit("scalar",
                       lambda s, tt=tt: s.dma_start(
                           xlnT[:, :, tt * 128:tt * 128 + 128],
                           xln[:, tt, :].rearrange("p (k j) -> p k j", j=128),
                           transpose=True),
                       waits=[("VEC", apd[tt])], inc="TRP", inc_by=16)
            pair_xT[p_] = P.count("TRP")
            prev_xT = pair_xT[p_]

        emit_pair_prefix(0)
        for p_ in range(npair):
            t0 = 4 * p_
            c_xT = pair_xT[p_]

            for e in range(ne):
                ge = p_ * ne + e
                for m in range(32):
                    gs = ge * NS1 + m // 4
                    pbk = 4 + m % 4
                    w0 = [w1slab_loaded[gs], ("TRP", c_xT)]
                    gm = ge * 32 + m
                    if gm >= 4:
                        w0.append(("SCE", gelu_cnt[gm - 4]))
                    if ge > 0 and m < 4:
                        w0.append(yev[(ge - 1, 2 + m // 2)])
                    first = True
                    for k in range(8):
                        cm = P.emit("tensor",
                                    lambda t, m=m, k=k, pbk=pbk, gs=gs: t.matmul(
                                        bank(pbk),
                                        w1r[:, gs % 2, m % 4, k, :],
                                        xlnT[:, k, :],
                                        start=(k == 0), stop=(k == 7)),
                                    waits=w0 if first else [], inc="TEN")
                        first = False
                    if m % 4 == 3:
                        w1slab_freed[gs] = cm
                        load_w1slab(gs + 2)
                    if m == 31 and e == ne - 1:
                        c_mm1_last[p_] = cm
                    gelu_cnt[gm] = P.emit("scalar",
                                          lambda s, m=m, pbk=pbk, e=e: s.activation(
                                              qkvB[:, m, :], bank(pbk), AF.Gelu,
                                              bias=b1c[:, e, m:m + 1]),
                                          waits=[("TEN", cm)], inc="SCE")
                c_gelu_all = gelu_cnt[ge * 32 + 31]
                frw = {}
                for r in range(4):
                    frw[r] = [("SCE", c_gelu_all)]
                    if ge > 0:
                        frw[r].append(yev[(ge - 1, r)])
                first = {r: True for r in range(4)}
                for k2 in range(32):
                    gs2 = ge * NS2 + k2 // 4
                    wk = [w2slab_loaded[gs2]]
                    for r in range(4):
                        cp, tw = divmod(r, 2)
                        for half in range(2):
                            wts = []
                            if first[r]:
                                wts = frw[r] + wk
                            elif half == 0 and r == 0 and k2 % 4 == 0:
                                wts = wk
                            cm = P.emit("tensor",
                                        lambda t, k2=k2, cp=cp, tw=tw, half=half, r=r, gs2=gs2: t.matmul(
                                            bank(2 * r + half),
                                            qkvB[:, k2, cp * 256 + tw * 128: cp * 256 + tw * 128 + 128],
                                            w2r[:, gs2 % 2, k2 % 4, 512 * half:512 * half + 512],
                                            start=(k2 == 0), stop=(k2 == 31)),
                                        waits=wts, inc="TEN")
                            first[r] = False
                    if k2 % 4 == 3:
                        w2slab_freed[gs2] = cm
                        load_w2slab(gs2 + 2)
                c_mm2 = cm
                for r in (2, 3, 0, 1):
                    if e == 0:
                        cy = P.emit("vector",
                                    lambda v, r=r: v.tensor_scalar(
                                        ybuf[:, r, :], PS[:, 1024 * r:1024 * r + 1024],
                                        wec[:, 0:1], None, op0=ALU.mult),
                                    waits=[("TEN", c_mm2)], inc="VEC")
                    else:
                        cy = P.emit("vector",
                                    lambda v, r=r, e=e: v.scalar_tensor_tensor(
                                        ybuf[:, r, :], PS[:, 1024 * r:1024 * r + 1024],
                                        wec[:, e:e + 1], ybuf[:, r, :],
                                        op0=ALU.mult, op1=ALU.add),
                                    waits=[("TEN", c_mm2)], inc="VEC")
                    yev[(ge, r)] = ("VEC", cy)
            prev_pair_ten = P.count("TEN")
            if p_ + 1 < npair:
                emit_pair_prefix(p_ + 1)

            ge_last = p_ * ne + ne - 1
            for r in range(4):
                gi = 4 * p_ + r
                cb = P.emit("vector",
                            lambda v, r=r: v.tensor_tensor(ybuf[:, r, :], ybuf[:, r, :], b2bc[:, :], op=ALU.add),
                            waits=[yev[(ge_last, r)]], inc="VEC")
                cs1 = P.emit("vector", lambda v, r=r: v.reduce_sum(cst[:, 28:29], ybuf[:, r, :], axis=AXX),
                             waits=[("VEC", cb)], inc="VEC")
                cq1 = P.emit("scalar",
                             lambda s, r=r: s.activation(
                                 qkvB[:, 0:2, :], ybuf[:, r, :].rearrange("p (a b) -> p a b", a=2),
                                 AF.Square, accum_out=cst[:, 29:30]),
                             waits=[("VEC", cb), ("SCE", P.count("SCE"))], inc="SCE")
                cm1 = P.emit("vector", lambda v: v.tensor_scalar(cst[:, 28:29], cst[:, 28:29], 1.0 / D, None, op0=ALU.mult),
                             waits=[("VEC", cs1)], inc="VEC")
                cm2 = P.emit("vector", lambda v: v.tensor_tensor(cst[:, 30:31], cst[:, 28:29], cst[:, 28:29], op=ALU.mult),
                             waits=[("VEC", cm1)], inc="VEC")
                cm3 = P.emit("vector",
                             lambda v: v.scalar_tensor_tensor(cst[:, 30:31], cst[:, 29:30], 1.0 / D,
                                                              cst[:, 30:31], op0=ALU.mult, op1=ALU.subtract),
                             waits=[("SCE", cq1), ("VEC", cm2)], inc="VEC")
                cln = P.emit("scalar", lambda s: s.activation(cst[:, 31:32], cst[:, 30:31], AF.Ln, bias=epsc[:, 0:1]),
                             waits=[("VEC", cm3)], inc="SCE")
                crs = P.emit("scalar", lambda s: s.activation(cst[:, 30:31], cst[:, 31:32], AF.Exp, scale=-0.5),
                             waits=[("SCE", cln)], inc="SCE")
                wa = [("SCE", crs)]
                if gi >= 2:
                    wa.append((f"ST{gi % 2}", 16 * ((gi - 2) // 2 + 1)))
                ca1 = P.emit("vector",
                             lambda v, r=r, gi=gi: v.tensor_scalar(outb[:, gi % 2, :], ybuf[:, r, :],
                                                                   cst[:, 28:29], cst[:, 30:31],
                                                                   op0=ALU.subtract, op1=ALU.mult),
                             waits=wa, inc="VEC")
                ca2 = P.emit("vector",
                             lambda v, r=r, t0=t0, gi=gi: v.tensor_tensor(outb[:, gi % 2, :], outb[:, gi % 2, :],
                                                                          R[:, t0 + r, :], op=ALU.add),
                             inc="VEC")
                P.emit("gpsimd",
                       lambda g, t0=t0, r=r, gi=gi: g.dma_start(dout[(t0 + r) * 128:(t0 + r + 1) * 128, :],
                                                                outb[:, gi % 2, :]),
                       waits=[("VEC", ca2)], inc=f"ST{gi % 2}", inc_by=16)

        P.emit("gpsimd", lambda g: None, waits=[("ST0", P.count("ST0")), ("ST1", P.count("ST1"))])
        P.emit("sync", lambda s: None, waits=[("ST0", P.count("ST0")), ("ST1", P.count("ST1"))])

        def replay(eng, name):
            for fn, waits, inc, inc_by in P.ops[name]:
                for (s, v) in waits:
                    eng.wait_ge(sems[s], v)
                inst = fn(eng)
                if inc is not None and inst is not None:
                    inst.then_inc(sems[inc], inc_by)

        @block.sync
        def _(s):
            replay(s, "sync")

        @block.gpsimd
        def _(g):
            replay(g, "gpsimd")

        @block.tensor
        def _(t):
            replay(t, "tensor")

        @block.scalar
        def _(s):
            replay(s, "scalar")

        @block.vector
        def _(v):
            replay(v, "vector")

    return nc


def _prep_weights(inputs):
    Wqkv = np.ascontiguousarray(inputs["L0_attn_W_qkv"])
    Wproj = np.ascontiguousarray(inputs["L0_attn_W_proj"])
    w1 = np.ascontiguousarray(inputs["L0_ffn_w1"])
    w2 = np.ascontiguousarray(inputs["L0_ffn_w2"])
    b1 = np.ascontiguousarray(inputs["L0_ffn_b1"])
    b2 = np.ascontiguousarray(inputs["L0_ffn_b2"])
    gate = np.ascontiguousarray(inputs["L0_ffn_gate_w"])
    g_ = inputs["L0_attn_ln_gamma"]
    be = inputs["L0_attn_ln_beta"]
    bp = inputs["L0_attn_b_proj"]

    out = {}
    out["wqkv"] = np.ascontiguousarray(Wqkv.reshape(8, 128, 24, 128).transpose(1, 2, 0, 3))
    out["wproj"] = np.ascontiguousarray(Wproj.reshape(8, 128, D).transpose(1, 0, 2))
    out["w1"] = np.ascontiguousarray(w1.reshape(8, 8, 128, 32, 128).transpose(0, 2, 3, 1, 4))
    out["w2"] = np.ascontiguousarray(w2.reshape(8, 32, 128, D).transpose(0, 2, 1, 3))
    out["b1"] = np.ascontiguousarray(b1.reshape(8, 32, 128).transpose(0, 2, 1))
    out["b2"] = b2
    out["gate"] = np.ascontiguousarray(gate.reshape(8, 128, 8).transpose(1, 0, 2))
    out["gbc"] = np.ascontiguousarray(np.broadcast_to(np.tile(g_, H)[None, :], (128, D)))
    out["bbc"] = np.ascontiguousarray(np.broadcast_to(np.tile(be, H)[None, :], (128, D)))
    out["pbc"] = np.ascontiguousarray(np.broadcast_to(bp[None, :], (128, D)))
    rotm = np.zeros((128, 128), np.float32)
    for m in range(128):
        rotm[(m + 64) % 128, m] = 1.0
    out["rot"] = rotm
    out["onescol"] = np.ones((128, 1), np.float32)
    out["onesrow"] = np.ones((1, 128), np.float32)
    return out


def kernel(**inputs):
    x = inputs["L0_attn_x"]
    wb = _prep_weights(inputs)
    nc = build_nc()
    in_maps = []
    for c in range(8):
        m = {"x": np.ascontiguousarray(x[c])}
        m.update(wb)
        in_maps.append(m)
    res = run_bass_kernel_spmd(nc, in_maps, core_ids=list(range(8)))
    out = np.stack([res.results[c]["out"] for c in range(8)], axis=0)
    return out.astype(np.float32)


# version 10
# speedup vs baseline: 1.1660x; 1.1660x over previous
import contextlib
import numpy as np

import concourse.bass as bass
import concourse.mybir as mybir
from concourse.bass_utils import run_bass_kernel_spmd

F32 = mybir.dt.float32
BF16 = mybir.dt.bfloat16
AF = mybir.ActivationFunctionType
ALU = mybir.AluOpType
AXX = mybir.AxisListType.X

D = 1024
H = 16
HD = 64
SCALE = 8.0 / HD
EPS = 1e-5
NE = 8


class Plan:
    def __init__(self, strict_edges=False):
        self.ops = {"tensor": [], "vector": [], "scalar": [], "sync": [], "gpsimd": []}
        self.counts = {}
        self.strict_edges = strict_edges

    ENG_SEM = {"vector": "VEC", "scalar": "SCE", "gpsimd": "PLE"}

    def emit(self, engine, fn, waits=(), inc=None, inc_by=1):
        w = [(s, v) for (s, v) in waits if v and v > 0]
        if self.strict_edges:
            es_ = self.ENG_SEM.get(engine)
            if es_ is not None:
                pc = self.counts.get(es_, 0)
                if pc > 0 and (es_, pc) not in w:
                    w.append((es_, pc))
        if inc is not None:
            self.counts[inc] = self.counts.get(inc, 0) + inc_by
        self.ops[engine].append((fn, tuple(w), inc, inc_by))
        return self.counts.get(inc, 0) if inc is not None else None

    def count(self, sem):
        return self.counts.get(sem, 0)


def build_nc(ntok=4096, nch_a=8, npair=8, ne=NE, attn_only=False, strict_edges=True):
    NT = ntok // 128
    nc = bass.Bass("TRN2", target_bir_lowering=False)

    dx = nc.dram_tensor("x", [ntok, D], F32, kind="ExternalInput").ap()
    dwqkv = nc.dram_tensor("wqkv", [128, 24, 8, 128], F32, kind="ExternalInput").ap()
    dwproj = nc.dram_tensor("wproj", [128, 8, D], F32, kind="ExternalInput").ap()
    dw1 = nc.dram_tensor("w1", [ne, 128, 32, 8, 128], F32, kind="ExternalInput").ap()
    dw2 = nc.dram_tensor("w2", [ne, 128, 32, D], F32, kind="ExternalInput").ap()
    db1 = nc.dram_tensor("b1", [ne, 128, 32], F32, kind="ExternalInput").ap()
    db2 = nc.dram_tensor("b2", [ne, D], F32, kind="ExternalInput").ap()
    dgate = nc.dram_tensor("gate", [128, 8, ne], F32, kind="ExternalInput").ap()
    dgbc = nc.dram_tensor("gbc", [128, D], F32, kind="ExternalInput").ap()
    dbbc = nc.dram_tensor("bbc", [128, D], F32, kind="ExternalInput").ap()
    dpbc = nc.dram_tensor("pbc", [128, D], F32, kind="ExternalInput").ap()
    drot = nc.dram_tensor("rot", [128, 128], F32, kind="ExternalInput").ap()
    dones = nc.dram_tensor("onescol", [128, 1], F32, kind="ExternalInput").ap()
    donesr = nc.dram_tensor("onesrow", [1, 128], F32, kind="ExternalInput").ap()
    dout = nc.dram_tensor("out", [ntok, D], F32, kind="ExternalOutput").ap()

    swqkv = nc.dram_tensor("s_wqkv", [128, 24, 8, 128], BF16, kind="Internal").ap()
    swproj = nc.dram_tensor("s_wproj", [128, 8, D], BF16, kind="Internal").ap()
    sw1 = nc.dram_tensor("s_w1", [ne, 128, 32, 8, 128], BF16, kind="Internal").ap()
    sw2 = nc.dram_tensor("s_w2", [ne, 128, 32, D], BF16, kind="Internal").ap()

    es = contextlib.ExitStack()
    with es:
        def sbuf(shape, dt, _n=[0]):
            _n[0] += 1
            return es.enter_context(nc.sbuf_tensor(f"sb{_n[0]}", shape, dt))

        R = sbuf([128, NT, D], BF16)
        qkvB = sbuf([128, 32, 512], BF16)
        xln = sbuf([128, 4, D], BF16)
        xlnT = sbuf([128, 8, 512], BF16)
        WS = sbuf([128, 18432], BF16)
        FP = sbuf([128, 4, D], F32)
        outb = sbuf([128, 2, D], F32)
        wqr = sbuf([128, 2, 2, 8, 128], BF16)
        wpr = sbuf([128, 2, 2, D], BF16)
        gbc = sbuf([128, D], BF16)
        bbc = sbuf([128, D], BF16)
        pbc = sbuf([128, D], F32)
        b2bc = sbuf([128, D], F32)
        b1c = sbuf([128, ne, 32], F32)
        wec = sbuf([128, ne], F32)
        gatet = sbuf([128, 8, ne], F32)
        rot = sbuf([128, 128], BF16)
        onescb = sbuf([128, 1], BF16)
        onesrf = sbuf([1, 128], F32)
        mu2c = sbuf([128, NT], F32)
        rs2c = sbuf([128, NT], F32)
        qks = sbuf([128, 256], F32)
        rsc = sbuf([128, 4 * H], F32)
        cst = sbuf([128, 32], F32)
        seqc = sbuf([128, 8], F32)
        wcolb = sbuf([8, 1], F32)
        epsc = sbuf([128, 1], F32)
        PS = es.enter_context(nc.psum_tensor([128, 4096], F32))

        qA = WS[:, 0:4096].rearrange("p (t d) -> p t d", t=4)
        kA = WS[:, 4096:8192].rearrange("p (t d) -> p t d", t=4)
        vA = WS[:, 8192:12288].rearrange("p (t d) -> p t d", t=4)
        vA2 = WS[:, 12288:14336].rearrange("p (s d) -> p s d", s=2)
        Pb = WS[:, 14336:16384].rearrange("p (s d) -> p s d", s=2)
        PTb = WS[:, 16384:18432].rearrange("p (s d) -> p s d", s=2)
        w1r = WS[:, 0:8192].rearrange("p (s m k j) -> p s m k j", s=2, m=4, k=8)
        w2r = WS[:, 8192:16384].rearrange("p (s k d) -> p s k d", s=2, k=4)
        b2s = outb[0:8, 0, :]
        rowb = FP[0:1, 3, :]
        rsmall = FP[0:1, 2, 0:128]
        scr = FP[:, 0, :]
        scr2 = FP[:, 1, :]
        aos = FP[:, 2, :]
        ybuf = FP

        sems = {}
        for name in ["GX", "GP", "CQK", "CPJ",
                     "WQ0", "WQ1", "WP0", "WP1", "W10", "W11", "W20", "W21",
                     "TRS", "TRP", "TP0", "TP1", "ST0", "ST1",
                     "TEN", "SCE", "VEC", "PLE", "TQA", "TKA", "TVA"] + [f"CE{e}" for e in range(NE)]:
            sems[name] = es.enter_context(nc.semaphore(name))

        block = es.enter_context(nc.Block())
        P = Plan(strict_edges=strict_edges)

        def bank(b, width=512):
            return PS[:, 512 * b: 512 * b + width]

        def gq(sem, fn, waits=()):
            P.emit("gpsimd", fn, waits=waits, inc=sem, inc_by=16)
            return P.count(sem)

        c_R = gq("GX", lambda g: g.dma_start(R[:, :, :], dx.rearrange("(t p) d -> p t d", p=128)))
        gq("GP", lambda g: g.dma_start(b2s[:, :], db2))
        gq("GP", lambda g: g.dma_start(gbc[:, :], dgbc))
        gq("GP", lambda g: g.dma_start(bbc[:, :], dbbc))
        gq("GP", lambda g: g.dma_start(pbc[:, :], dpbc))
        gq("GP", lambda g: g.dma_start(rot[:, :], drot))
        gq("GP", lambda g: g.dma_start(onescb[:, :], dones))
        gq("GP", lambda g: g.dma_start(onesrf[:, :], donesr))
        gq("GP", lambda g: g.dma_start(b1c[:, :, :], db1.rearrange("e p m -> p e m")))
        c_params = gq("GP", lambda g: g.dma_start(gatet[:, :, :], dgate))
        c_wqkv = gq("CQK", lambda g: g.dma_start(swqkv, dwqkv))
        c_wproj = gq("CPJ", lambda g: g.dma_start(swproj, dwproj))
        c_wexp = []
        for e in range(ne):
            gq(f"CE{e}", lambda g, e=e: g.dma_start(sw1[e], dw1[e]))
            c_wexp.append(gq(f"CE{e}", lambda g, e=e: g.dma_start(sw2[e], dw2[e])))

        P.emit("vector", lambda v: v.memset(epsc[:, :], EPS),
               waits=[("GP", c_params)], inc="VEC")

        pslab_loaded = {}
        pslab_freed = {}

        def load_pslab(g):
            if g >= nch_a * 16:
                return
            s_ = g % 4
            w = [("CPJ", c_wproj)]
            if g >= 2:
                w.append(("TEN", pslab_freed[g - 2]))
            cnt = P.emit(
                "sync",
                lambda s, s_=s_, g=g: s.dma_start(wpr[:, g % 2, :, :],
                                                  swproj[:, 2 * s_:2 * s_ + 2, :]),
                waits=w, inc=f"WP{g % 2}", inc_by=16)
            pslab_loaded[g] = (f"WP{g % 2}", cnt)

        load_pslab(0)
        load_pslab(1)

        NSLAB_Q = 12
        qslab_loaded = {}
        qslab_freed = {}

        def load_qslab(g):
            c, s_ = divmod(g, NSLAB_Q)
            if c >= nch_a:
                return
            w = [("CQK", c_wqkv)]
            if g >= 2:
                w.append(("TEN", qslab_freed[g - 2]))
            cnt = P.emit(
                "sync",
                lambda s, s_=s_, g=g: s.dma_start(wqr[:, g % 2, :, :, :],
                                                  swqkv[:, 2 * s_:2 * s_ + 2, :, :]),
                waits=w, inc=f"WQ{g % 2}", inc_by=16)
            qslab_loaded[g] = (f"WQ{g % 2}", cnt)

        load_qslab(0)
        load_qslab(1)

        prev_xlnT = 0
        prev_avT = 0
        prev_qkv_end = 0
        prev_chunk_vec = 0
        prev_vrot_evac = None
        prev_ttsub = None
        prev_cexp = None
        av_done = {}
        av_ev = {}
        pt_cnt = {}

        chunk_xlnT = {}

        def emit_prologue(cc):
            nonlocal prev_xlnT
            t0p = 4 * cc
            vec_h = P.count("VEC")
            for tt in range(4):
                P.emit("vector",
                       lambda v, tt=tt, t0p=t0p: v.reduce_sum(cst[:, tt:tt + 1], R[:, t0p + tt, :], axis=AXX),
                       waits=[("GX", c_R)] if (cc == 0 and tt == 0) else [], inc="VEC")
            c_sum = P.count("VEC")
            for tt in range(4):
                wsq = [("SCE", P.count("SCE"))]
                if cc == 0 and tt == 0:
                    wsq.append(("GX", c_R))
                if tt == 0 and cc > 0:
                    wsq.append(("VEC", vec_h))
                P.emit("scalar",
                       lambda s, tt=tt, t0p=t0p: s.activation(scr[:, :], R[:, t0p + tt, :], AF.Square,
                                                              accum_out=cst[:, 4 + tt:5 + tt]),
                       waits=wsq, inc="SCE")
            c_sq = P.count("SCE")
            P.emit("vector", lambda v: v.tensor_scalar(cst[:, 8:12], cst[:, 0:4], 1.0 / D, None, op0=ALU.mult),
                   waits=[("VEC", c_sum)], inc="VEC")
            P.emit("vector", lambda v: v.tensor_tensor(cst[:, 12:16], cst[:, 8:12], cst[:, 8:12], op=ALU.mult),
                   inc="VEC")
            c_v1 = P.emit("vector",
                          lambda v: v.scalar_tensor_tensor(cst[:, 16:20], cst[:, 4:8], 1.0 / D,
                                                           cst[:, 12:16], op0=ALU.mult, op1=ALU.subtract),
                          waits=[("SCE", c_sq)], inc="VEC")
            c_ln1 = P.emit("scalar", lambda s: s.activation(cst[:, 20:24], cst[:, 16:20], AF.Ln, bias=epsc[:, 0:1]),
                           waits=[("VEC", c_v1)], inc="SCE")
            c_rs = P.emit("scalar", lambda s: s.activation(cst[:, 24:28], cst[:, 20:24], AF.Exp, scale=-0.5),
                          waits=[("SCE", c_ln1)], inc="SCE")
            ap_done = []
            for tt in range(4):
                w = [("SCE", c_rs)]
                if tt == 0 and cc > 0:
                    w += [("TRP", prev_xlnT), ("TRS", prev_avT)]
                cv = P.emit("vector",
                            lambda v, tt=tt, t0p=t0p: v.tensor_scalar(
                                xln[:, tt, :], R[:, t0p + tt, :],
                                cst[:, 8 + tt:9 + tt], cst[:, 24 + tt:25 + tt],
                                op0=ALU.subtract, op1=ALU.mult),
                            waits=w if tt == 0 else [], inc="VEC")
                ap_done.append(cv)
            return ap_done

        def emit_prologue_T(cc, ap_done):
            nonlocal prev_xlnT
            for tt in range(4):
                w = [("VEC", ap_done[tt])]
                if tt == 0 and cc > 0:
                    w.append(("TEN", prev_qkv_end))
                P.emit("scalar",
                       lambda s, tt=tt: s.dma_start(
                           xlnT[:, :, tt * 128:tt * 128 + 128],
                           xln[:, tt, :].rearrange("p (k j) -> p k j", j=128),
                           transpose=True),
                       waits=w, inc="TRP", inc_by=16)
            chunk_xlnT[cc] = P.count("TRP")
            prev_xlnT = chunk_xlnT[cc]

        emit_prologue_T(0, emit_prologue(0))
        for c in range(nch_a):
            t0 = 4 * c
            c_xlnT = chunk_xlnT[c]

            evac_cnt = {}
            grp_T = {}
            for m in range(24):
                g = c * NSLAB_Q + m // 2
                w = [qslab_loaded[g], ("TRP", c_xlnT)]
                if m >= 2:
                    w.append(evac_cnt[m - 2])
                for k in range(8):
                    cm = P.emit("tensor",
                                lambda t, m=m, k=k, g=g: t.matmul(
                                    bank(m % 2),
                                    wqr[:, g % 2, m % 2, k, :],
                                    xlnT[:, k, :],
                                    start=(k == 0), stop=(k == 7)),
                                waits=w if k == 0 else [], inc="TEN")
                if m % 2 == 1:
                    qslab_freed[g] = cm
                    load_qslab(g + 2)
                ce = P.emit("scalar",
                            lambda s, m=m: s.activation(qkvB[:, m, :], bank(m % 2), AF.Copy),
                            waits=[("TEN", cm)], inc="SCE")
                evac_cnt[m] = ("SCE", ce)
                dst = (qA, kA, vA)[m // 8]
                gsem = ("TQA", "TKA", "TVA")[m // 8]
                P.emit("sync",
                       lambda s, m=m, dst=dst: s.dma_start(
                           dst[:, :, (m % 8) * 128:(m % 8) * 128 + 128],
                           qkvB[:, m, :].rearrange("p (t j) -> p t j", j=128),
                           transpose=True),
                       waits=[evac_cnt[m]], inc=gsem, inc_by=16)
                if m % 8 == 7:
                    grp_T[m // 8] = (gsem, P.count(gsem))
            prev_qkv_end = cm
            c_vA = grp_T[2]

            red_of = {}
            for si in range(2):
                src = (qA, kA)[si]
                for tt in range(4):
                    o = si * 128 + tt * 16
                    P.emit("vector",
                           lambda v, src=src, tt=tt, o=o: v.tensor_reduce(
                               qks[:, o:o + 16].rearrange("p (h x) -> p h x", x=1),
                               src[:, tt, :].rearrange("p (h x) -> p h x", x=64),
                               axis=AXX, op=ALU.add),
                           waits=[grp_T[si]] if tt == 0 else [], inc="VEC")
            prev_red = None
            for si in range(2):
                src = (qA, kA)[si]
                for tt in range(4):
                    i8 = si * 4 + tt
                    o = si * 128 + tt * 16
                    sq_dst = scr if i8 % 2 == 0 else scr2
                    wq = [grp_T[si]] if tt == 0 else []
                    wq.append(("SCE", P.count("SCE")))
                    if i8 == 0 and c > 0:
                        wq.append(("VEC", prev_chunk_vec))
                    if i8 >= 2:
                        wq.append(("VEC", red_of[i8 - 2]))
                    cq = P.emit("scalar",
                                lambda s, src=src, tt=tt, sq_dst=sq_dst: s.activation(
                                    sq_dst[:, :], src[:, tt, :], AF.Square),
                                waits=wq, inc="SCE")
                    prev_red = P.emit("vector",
                                      lambda v, o=o, sq_dst=sq_dst: v.tensor_reduce(
                                          qks[:, 64 + o:64 + o + 16].rearrange("p (h x) -> p h x", x=1),
                                          sq_dst[:, :].rearrange("p (h x) -> p h x", x=64),
                                          axis=AXX, op=ALU.add),
                                      waits=[("SCE", cq)], inc="VEC")
                    red_of[i8] = prev_red
            c_qs = prev_red
            P.emit("vector", lambda v: v.tensor_scalar(qks[:, 0:64], qks[:, 0:64], 1.0 / HD, None, op0=ALU.mult),
                   waits=[("VEC", c_qs)], inc="VEC")
            P.emit("vector", lambda v: v.tensor_scalar(qks[:, 128:192], qks[:, 128:192], 1.0 / HD, None, op0=ALU.mult),
                   inc="VEC")
            P.emit("vector", lambda v: v.tensor_tensor(scr2[:, 0:64], qks[:, 0:64], qks[:, 0:64], op=ALU.mult),
                   inc="VEC")
            P.emit("vector", lambda v: v.tensor_tensor(scr2[:, 64:128], qks[:, 128:192], qks[:, 128:192], op=ALU.mult),
                   inc="VEC")
            P.emit("vector",
                   lambda v: v.scalar_tensor_tensor(qks[:, 64:128], qks[:, 64:128], 1.0 / HD,
                                                    scr2[:, 0:64], op0=ALU.mult, op1=ALU.subtract),
                   inc="VEC")
            cvv = P.emit("vector",
                         lambda v: v.scalar_tensor_tensor(qks[:, 192:256], qks[:, 192:256], 1.0 / HD,
                                                          scr2[:, 64:128], op0=ALU.mult, op1=ALU.subtract),
                         inc="VEC")
            cl1 = P.emit("scalar", lambda s: s.activation(scr2[:, 0:64],
                                                          qks[:, 64:128], AF.Ln, bias=epsc[:, 0:1]),
                         waits=[("VEC", cvv)], inc="SCE")
            cl2 = P.emit("scalar", lambda s: s.activation(qks[:, 64:128], scr2[:, 0:64], AF.Exp, scale=-0.5),
                         waits=[("SCE", cl1)], inc="SCE")
            cl3 = P.emit("scalar", lambda s: s.activation(scr2[:, 128:192], qks[:, 192:256], AF.Ln, bias=epsc[:, 0:1]),
                         inc="SCE")
            c_qkrs = P.emit("scalar", lambda s: s.activation(qks[:, 192:256], scr2[:, 128:192], AF.Exp, scale=-0.5),
                            waits=[("SCE", cl3)], inc="SCE")

            qk_ap = {}
            for si in range(2):
                src = (qA, kA)[si]
                for tp_ in range(2):
                    on_pool = (tp_ == 1)
                    eng = "gpsimd" if on_pool else "vector"
                    esem = "PLE" if on_pool else "VEC"
                    sscr = aos if on_pool else scr
                    om = si * 128 + tp_ * 32
                    ors = om + 64
                    w0 = [("SCE", c_qkrs if si == 1 else cl2)]
                    P.emit(eng,
                           lambda v, src=src, tp_=tp_, om=om, sscr=sscr: v.tensor_tensor(
                               sscr[:, :].rearrange("p (h j) -> p h j", j=64),
                               src[:, 2 * tp_, :].rearrange("p (h j) -> p h j", j=64),
                               qks[:, om:om + 16].unsqueeze(-1).broadcast_to([128, 16, 64]),
                               op=ALU.subtract),
                           waits=w0 if (tp_ == 0 or on_pool) else [], inc=esem)
                    P.emit(eng,
                           lambda v, ors=ors, sscr=sscr: v.tensor_tensor(
                               sscr[:, :].rearrange("p (h j) -> p h j", j=64),
                               sscr[:, :].rearrange("p (h j) -> p h j", j=64),
                               qks[:, ors:ors + 16].unsqueeze(-1).broadcast_to([128, 16, 64]),
                               op=ALU.mult),
                           inc=esem)
                    P.emit(eng,
                           lambda v, sscr=sscr: v.tensor_tensor(sscr[:, :], sscr[:, :], gbc[:, :], op=ALU.mult),
                           inc=esem)
                    ca_t0 = P.emit(eng,
                                   lambda v, src=src, tp_=tp_, sscr=sscr: v.tensor_tensor(
                                       src[:, 2 * tp_, :], sscr[:, :], bbc[:, :], op=ALU.add),
                                   inc=esem)
                    qk_ap[(si, 2 * tp_)] = (esem, ca_t0)
                    om2 = om + 16
                    ors2 = om2 + 64
                    P.emit(eng,
                           lambda v, src=src, tp_=tp_, om2=om2, sscr=sscr: v.tensor_tensor(
                               sscr[:, :].rearrange("p (h j) -> p h j", j=64),
                               src[:, 2 * tp_ + 1, :].rearrange("p (h j) -> p h j", j=64),
                               qks[:, om2:om2 + 16].unsqueeze(-1).broadcast_to([128, 16, 64]),
                               op=ALU.subtract),
                           inc=esem)
                    P.emit(eng,
                           lambda v, ors2=ors2, sscr=sscr: v.tensor_tensor(
                               sscr[:, :].rearrange("p (h j) -> p h j", j=64),
                               sscr[:, :].rearrange("p (h j) -> p h j", j=64),
                               qks[:, ors2:ors2 + 16].unsqueeze(-1).broadcast_to([128, 16, 64]),
                               op=ALU.mult),
                           inc=esem)
                    P.emit(eng,
                           lambda v, sscr=sscr: v.tensor_tensor(sscr[:, :], sscr[:, :], gbc[:, :], op=ALU.mult),
                           inc=esem)
                    ca = P.emit(eng,
                                lambda v, src=src, tp_=tp_, sscr=sscr: v.tensor_tensor(
                                    src[:, 2 * tp_ + 1, :], sscr[:, :], bbc[:, :], op=ALU.add),
                                inc=esem)
                    qk_ap[(si, 2 * tp_ + 1)] = (esem, ca)

            for si in range(2):
                srcb = (qA, kA)[si]
                for tt in range(4):
                    P.emit("sync",
                           lambda s, si=si, srcb=srcb, tt=tt: s.dma_start(
                               qkvB[:, si * 8:si * 8 + 8, tt * 128:tt * 128 + 128],
                               srcb[:, tt, :].rearrange("p (k j) -> p k j", j=128),
                               transpose=True),
                           waits=[qk_ap[(si, tt)]],
                           inc="TRS", inc_by=16)
            c_qkT = P.count("TRS")

            score4d_in = PS[:, 1024:3072].rearrange("p (b x) -> p b x", b=4)[:, :, 0:256] \
                .rearrange("p b (g j) -> p b g j", j=64)
            for wp in range(4):
                gwp = 4 * c + wp
                slot = gwp % 2
                w0 = [("TRS", c_qkT)]
                if wp == 0 and c > 0:
                    w0.append(("VEC", prev_chunk_vec))
                if prev_ttsub is not None:
                    w0.append(("VEC", prev_ttsub))
                if prev_vrot_evac is not None:
                    w0.append(prev_vrot_evac)
                first = True
                for hh in range(H):
                    mq = hh // 2
                    pbase = 64 * (hh % 2)
                    colq = 1024 + 512 * (hh % 4) + 64 * (hh // 4)
                    for wpar in range(2):
                        cm = P.emit("tensor",
                                    lambda t, mq=mq, pbase=pbase, colq=colq, wpar=wpar, wp=wp: t.matmul(
                                        PS[64 * wpar:64 * wpar + 64, colq:colq + 64],
                                        qkvB[pbase:pbase + 64, mq, wp * 128 + 64 * wpar: wp * 128 + 64 * wpar + 64],
                                        qkvB[pbase:pbase + 64, 8 + mq, wp * 128 + 64 * wpar: wp * 128 + 64 * wpar + 64],
                                        start=True, stop=True,
                                        tile_position=(pbase, 64 * wpar)),
                                    waits=w0 if first else [], inc="TEN")
                        first = False
                c_sc = cm
                cmx = P.emit("vector",
                             lambda v: v.tensor_reduce(
                                 rsc[:, 0:16].rearrange("p (b g x) -> p b g x", g=4, x=1),
                                 score4d_in, axis=AXX, op=ALU.max, negate=True),
                             waits=[("TEN", c_sc)], inc="VEC")
                wsub = [("VEC", cmx)]
                if prev_cexp is not None:
                    wsub.append(("SCE", prev_cexp))
                csub = P.emit("vector",
                              lambda v: v.tensor_tensor(
                                  scr2[:, :].rearrange("p (b g j) -> p b g j", b=4, j=64),
                                  score4d_in,
                                  rsc[:, 0:16].rearrange("p (b g) -> p b g", g=4).unsqueeze(-1)
                                  .broadcast_to([128, 4, 4, 64]),
                                  op=ALU.add),
                              waits=wsub, inc="VEC")
                prev_ttsub = csub
                wexp = [("VEC", csub)]
                if gwp >= 2:
                    wexp.append(pt_cnt[gwp - 2])
                cexp = P.emit("scalar",
                              lambda s, slot=slot: s.activation(Pb[:, slot, :], scr2[:, :], AF.Exp, scale=SCALE),
                              waits=wexp, inc="SCE")
                prev_cexp = cexp
                csm = P.emit("vector",
                             lambda v, slot=slot: v.tensor_reduce(
                                 rsc[:, 16 + 16 * slot:32 + 16 * slot].rearrange("p (h x) -> p h x", x=1),
                                 Pb[:, slot, :].rearrange("p (h j) -> p h j", j=64),
                                 axis=AXX, op=ALU.add),
                             waits=[("SCE", cexp)], inc="VEC")
                crc = P.emit("vector",
                             lambda v, slot=slot: v.reciprocal(rsc[:, 16 + 16 * slot:32 + 16 * slot],
                                                               rsc[:, 16 + 16 * slot:32 + 16 * slot]),
                             waits=[("VEC", csm)], inc="VEC")
                w1_ = [("VEC", csub), c_vA]
                if prev_vrot_evac is not None:
                    w1_.append(prev_vrot_evac)
                if gwp >= 2:
                    w1_.append(("TEN", av_done[gwp - 2]))
                P.emit("tensor",
                       lambda t, wp=wp: t.matmul(bank(2), rot[:, :], vA[:, wp, 0:512],
                                                 start=True, stop=True),
                       waits=w1_, inc="TEN")
                cr2 = P.emit("tensor",
                             lambda t, wp=wp: t.matmul(bank(3), rot[:, :], vA[:, wp, 512:1024],
                                                       start=True, stop=True),
                             inc="TEN")
                cve = P.emit("scalar",
                             lambda s, slot=slot: s.activation(vA2[:, slot, :], PS[:, 1024:2048], AF.Copy),
                             waits=[("TEN", cr2)], inc="SCE")
                prev_vrot_evac = ("SCE", cve)
                wt = [("SCE", cexp)]
                if gwp >= 2:
                    wt.append(("TEN", av_done[gwp - 2]))
                P.emit("scalar",
                       lambda s, slot=slot: s.dma_start(
                           PTb[:, slot, :].rearrange("p (a j) -> p a j", j=128),
                           Pb[:, slot, :].rearrange("p (a j) -> p a j", j=128),
                           transpose=True),
                       waits=wt, inc=f"TP{gwp % 2}", inc_by=16)
                pt_cnt[gwp] = (f"TP{gwp % 2}", P.count(f"TP{gwp % 2}"))
                avw = [pt_cnt[gwp], ("SCE", cve), c_vA]
                if gwp >= 2:
                    avw.append(av_ev[gwp - 2])
                first = True
                for s_ in range(H):
                    pbank = (0, 1, 6, 7)[s_ % 4]
                    colv = 512 * pbank + 64 * (s_ // 4)
                    ktbase = 64 * (s_ % 2)
                    jt = s_ // 2
                    h_true = 4 * (s_ % 4) + s_ // 4
                    for wpar in range(2):
                        vsel = ((s_ % 2) == wpar)
                        cm = P.emit("tensor",
                                    lambda t, wp=wp, slot=slot, jt=jt, ktbase=ktbase, wpar=wpar,
                                    h_true=h_true, colv=colv, vsel=vsel: t.matmul(
                                        PS[64 * wpar:64 * wpar + 64, colv:colv + 64],
                                        PTb[ktbase:ktbase + 64, slot, jt * 128 + 64 * wpar: jt * 128 + 64 * wpar + 64],
                                        (vA[ktbase:ktbase + 64, wp, h_true * 64:h_true * 64 + 64] if vsel
                                         else vA2[ktbase:ktbase + 64, slot, h_true * 64:h_true * 64 + 64]),
                                        start=True, stop=True,
                                        tile_position=(ktbase, 64 * wpar)),
                                    waits=avw if first else [], inc="TEN")
                        first = False
                av_done[gwp] = cm
                last = None
                for gsel in range(2):
                    boff = 0 if gsel == 0 else 3072
                    last = P.emit("vector",
                                  lambda v, wp=wp, slot=slot, boff=boff, gsel=gsel: v.tensor_tensor(
                                      xln[:, wp, :].rearrange("p (x y j) -> p x y j", x=4, j=64)
                                      [:, 2 * gsel:2 * gsel + 2, :, :],
                                      PS[:, boff:boff + 1024].rearrange("p (b x) -> p b x", b=2)[:, :, 0:256]
                                      .rearrange("p b (g j) -> p b g j", j=64),
                                      rsc[:, 16 + 16 * slot:32 + 16 * slot]
                                      .rearrange("p (y x) -> p x y", x=4)[:, 2 * gsel:2 * gsel + 2, :]
                                      .unsqueeze(-1).broadcast_to([128, 2, 4, 64]),
                                      op=ALU.mult),
                                  waits=[("TEN", av_done[gwp]), ("VEC", crc)], inc="VEC")
                av_ev[gwp] = ("VEC", last)

            for tt in range(4):
                P.emit("sync",
                       lambda s, tt=tt: s.dma_start(
                           qkvB[:, 16:24, tt * 128:tt * 128 + 128],
                           xln[:, tt, :].rearrange("p (k j) -> p k j", j=128),
                           transpose=True),
                       waits=[av_ev[4 * c + tt]], inc="TRS", inc_by=16)
            c_avT = P.count("TRS")
            prev_avT = c_avT

            pend_ap = emit_prologue(c + 1) if c + 1 < nch_a else None

            epi = {}
            for tt in range(4):
                pb0 = 2 + 2 * (tt % 2)
                w0 = [("TRS", c_avT)]
                if tt == 0:
                    w0.append(("VEC", prev_ttsub))
                    w0.append(prev_vrot_evac)
                if tt >= 2:
                    w0.append(epi[tt - 2])
                first = True
                for k in range(8):
                    gp = (c * 4 + tt) * 4 + k // 2
                    wk = [pslab_loaded[gp]]
                    for half in range(2):
                        wts = (w0 + wk) if first else (wk if (half == 0 and k % 2 == 0) else [])
                        cm = P.emit("tensor",
                                    lambda t, tt=tt, k=k, half=half, pb0=pb0, gp=gp: t.matmul(
                                        bank(pb0 + half),
                                        qkvB[:, 16 + k, tt * 128:tt * 128 + 128],
                                        wpr[:, gp % 2, k % 2, 512 * half:512 * half + 512],
                                        start=(k == 0), stop=(k == 7)),
                                    waits=wts, inc="TEN")
                        first = False
                    if k % 2 == 1:
                        pslab_freed[gp] = cm
                        load_pslab(gp + 2)
                ce1 = P.emit("vector",
                             lambda v, pb0=pb0: v.tensor_tensor(aos[:, :], PS[:, 512 * pb0:512 * pb0 + 1024],
                                                                pbc[:, :], op=ALU.add),
                             waits=[("TEN", cm), ("GP", c_params)], inc="VEC")
                cs1 = P.emit("vector", lambda v: v.reduce_sum(cst[:, 28:29], aos[:, :], axis=AXX),
                             waits=[("VEC", ce1)], inc="VEC")
                cq1 = P.emit("scalar", lambda s: s.activation(scr[:, :], aos[:, :], AF.Square,
                                                              accum_out=cst[:, 29:30]),
                             waits=[("VEC", ce1), ("SCE", P.count("SCE"))], inc="SCE")
                cm1 = P.emit("vector", lambda v: v.tensor_scalar(cst[:, 28:29], cst[:, 28:29], 1.0 / D, None, op0=ALU.mult),
                             waits=[("VEC", cs1)], inc="VEC")
                cm2 = P.emit("vector", lambda v: v.tensor_tensor(cst[:, 30:31], cst[:, 28:29], cst[:, 28:29], op=ALU.mult),
                             waits=[("VEC", cm1)], inc="VEC")
                cm3 = P.emit("vector",
                             lambda v: v.scalar_tensor_tensor(cst[:, 30:31], cst[:, 29:30], 1.0 / D,
                                                              cst[:, 30:31], op0=ALU.mult, op1=ALU.subtract),
                             waits=[("SCE", cq1), ("VEC", cm2)], inc="VEC")
                cln = P.emit("scalar", lambda s: s.activation(cst[:, 31:32], cst[:, 30:31], AF.Ln, bias=epsc[:, 0:1]),
                             waits=[("VEC", cm3)], inc="SCE")
                crs = P.emit("scalar", lambda s: s.activation(cst[:, 30:31], cst[:, 31:32], AF.Exp, scale=-0.5),
                             waits=[("SCE", cln)], inc="SCE")
                ca1 = P.emit("vector",
                             lambda v: v.tensor_scalar(scr2[:, :], aos[:, :], cst[:, 28:29], cst[:, 30:31],
                                                       op0=ALU.subtract, op1=ALU.mult),
                             waits=[("SCE", crs)], inc="VEC")
                ca2 = P.emit("vector",
                             lambda v, tt=tt, t0=t0: v.tensor_tensor(R[:, t0 + tt, :], scr2[:, :],
                                                                     R[:, t0 + tt, :], op=ALU.add),
                             inc="VEC")
                epi[tt] = ("VEC", ca2)
            if pend_ap is not None:
                emit_prologue_T(c + 1, pend_ap)
            prev_chunk_vec = P.count("VEC")

        if attn_only:
            npair = 0
        c_epi_all = P.count("VEC")
        prev_sq = None
        for t_ in (range(NT) if npair else []):
            P.emit("vector", lambda v, t_=t_: v.reduce_sum(mu2c[:, t_:t_ + 1], R[:, t_, :], axis=AXX),
                   waits=[("VEC", c_epi_all)] if t_ == 0 else [], inc="VEC")
            prev_sq = P.emit("scalar", lambda s, t_=t_: s.activation(scr[:, :], R[:, t_, :], AF.Square,
                                                                    accum_out=rs2c[:, t_:t_ + 1]),
                             waits=([("VEC", c_epi_all)] if t_ == 0 else []) + [("SCE", P.count("SCE"))],
                             inc="SCE")
        c_s1 = P.count("VEC")
        c_s2 = P.count("SCE")
        if npair == 0:
            for t_ in range(NT):
                cflush = P.emit("vector",
                                lambda v, t_=t_: v.tensor_copy(outb[:, t_ % 2, :], R[:, t_, :]),
                                waits=[("VEC", c_epi_all)] + ([(f"ST{t_ % 2}", 16 * ((t_ - 2) // 2 + 1))] if t_ >= 2 else []),
                                inc="VEC")
                P.emit("gpsimd",
                       lambda g, t_=t_: g.dma_start(dout[t_ * 128:(t_ + 1) * 128, :], outb[:, t_ % 2, :]),
                       waits=[("VEC", cflush)], inc=f"ST{t_ % 2}", inc_by=16)
            P.emit("gpsimd", lambda g: None, waits=[("ST0", P.count("ST0")), ("ST1", P.count("ST1"))])
            P.emit("sync", lambda s: None, waits=[("ST0", P.count("ST0")), ("ST1", P.count("ST1"))])
        else:
            _dummy = 0
        if npair:
            P.emit("vector", lambda v: v.tensor_scalar(mu2c[:, :], mu2c[:, :], 1.0 / D, None, op0=ALU.mult),
                   waits=[("VEC", c_s1)], inc="VEC")
            P.emit("vector", lambda v: v.tensor_tensor(cst[:, 0:NT], mu2c[:, :], mu2c[:, :], op=ALU.mult),
                   inc="VEC")
            cv2 = P.emit("vector",
                         lambda v: v.scalar_tensor_tensor(rs2c[:, :], rs2c[:, :], 1.0 / D, cst[:, 0:NT],
                                                          op0=ALU.mult, op1=ALU.subtract),
                         waits=[("SCE", c_s2)], inc="VEC")
            cln2 = P.emit("scalar", lambda s: s.activation(rs2c[:, :], rs2c[:, :], AF.Ln, bias=epsc[:, 0:1]),
                          waits=[("VEC", cv2)], inc="SCE")
            c_rs2 = P.emit("scalar", lambda s: s.activation(rs2c[:, :], rs2c[:, :], AF.Exp, scale=-0.5),
                           waits=[("SCE", cln2)], inc="SCE")
            prev_mm = None
            for t_ in range(NT):
                w = [("SCE", c_rs2)]
                if prev_mm is not None and t_ >= 2:
                    w.append(("TEN", prev_mm))
                cap = P.emit("vector",
                             lambda v, t_=t_: v.tensor_scalar(xln[:, t_ % 2, :], R[:, t_, :],
                                                              mu2c[:, t_:t_ + 1], rs2c[:, t_:t_ + 1],
                                                              op0=ALU.subtract, op1=ALU.mult),
                             waits=w, inc="VEC")
                for half in range(2):
                    prev_mm = P.emit("tensor",
                                     lambda t, t_=t_, half=half: t.matmul(
                                         PS[0:1, 512 * half:512 * half + 512],
                                         onescb[:, :],
                                         xln[:, t_ % 2, 512 * half:512 * half + 512],
                                         start=(t_ == 0), stop=(t_ == NT - 1)),
                                     waits=[("VEC", cap)] if half == 0 else [], inc="TEN")
            c_seq = prev_mm
            cr0 = P.emit("vector", lambda v: v.tensor_copy(rowb[0:1, 0:1024], PS[0:1, 0:1024]),
                         waits=[("TEN", c_seq)], inc="VEC")
            for k in range(8):
                cm = P.emit("tensor",
                            lambda t, k=k: t.matmul(PS[:, 512 * k + 256:512 * k + 257],
                                                    rowb[0:1, 128 * k:128 * k + 128],
                                                    onesrf[0:1, 0:1], start=True, stop=True),
                            waits=[("VEC", cr0)] if k == 0 else [], inc="TEN")
            c_sqc = cm
            cg = P.emit("vector",
                        lambda v: v.tensor_scalar(
                            seqc[:, :],
                            PS[:, :].rearrange("p (k x) -> p k x", x=512)[:, :, 256:257]
                            .rearrange("p k x -> p (k x)"),
                            1.0 / ntok, None, op0=ALU.mult),
                        waits=[("TEN", c_sqc)], inc="VEC")
            for k in range(8):
                cm = P.emit("tensor",
                            lambda t, k=k: t.matmul(PS[0:1, 0:ne], seqc[:, k:k + 1], gatet[:, k, :],
                                                    start=(k == 0), stop=(k == 7)),
                            waits=[("VEC", cg)] if k == 0 else [], inc="TEN")
            c_gate = cm
            LO, NM, EX, SM, PR, M1, PM, M2, KP, WR = 0, 16, 24, 32, 40, 48, 56, 64, 72, 80
            r_ = rsmall
            c1 = P.emit("vector", lambda v: v.tensor_copy(r_[0:1, LO:LO + ne], PS[0:1, 0:ne]),
                        waits=[("TEN", c_gate)], inc="VEC")
            c2 = P.emit("vector", lambda v: v.tensor_reduce(r_[0:1, NM:NM + 1], r_[0:1, LO:LO + ne],
                                                            axis=AXX, op=ALU.max, negate=True),
                        waits=[("VEC", c1)], inc="VEC")
            c2b = P.emit("scalar", lambda s: s.activation(r_[0:1, NM + 1:NM + 2], r_[0:1, NM:NM + 1], AF.Copy),
                         waits=[("VEC", c2)], inc="SCE")
            c3 = P.emit("vector", lambda v: v.tensor_scalar(r_[0:1, EX:EX + ne], r_[0:1, LO:LO + ne],
                                                            r_[0:1, NM + 1:NM + 2], None, op0=ALU.add),
                        waits=[("SCE", c2b)], inc="VEC")
            c4 = P.emit("scalar", lambda s: s.activation(r_[0:1, SM:SM + ne], r_[0:1, EX:EX + ne], AF.Exp),
                        waits=[("VEC", c3)], inc="SCE")
            c5 = P.emit("vector", lambda v: v.tensor_reduce(r_[0:1, M1:M1 + 1], r_[0:1, SM:SM + ne], axis=AXX, op=ALU.add),
                        waits=[("SCE", c4)], inc="VEC")
            c5b = P.emit("scalar", lambda s: s.activation(r_[0:1, M1 + 2:M1 + 3], r_[0:1, M1:M1 + 1], AF.Copy),
                         waits=[("VEC", c5)], inc="SCE")
            c6 = P.emit("vector", lambda v: v.reciprocal(r_[0:1, M1 + 1:M1 + 2], r_[0:1, M1 + 2:M1 + 3]),
                        waits=[("SCE", c5b)], inc="VEC")
            c6b = P.emit("scalar", lambda s: s.activation(r_[0:1, M1 + 3:M1 + 4], r_[0:1, M1 + 1:M1 + 2], AF.Copy),
                         waits=[("VEC", c6)], inc="SCE")
            c7 = P.emit("vector", lambda v: v.tensor_scalar(r_[0:1, PR:PR + ne], r_[0:1, SM:SM + ne],
                                                            r_[0:1, M1 + 3:M1 + 4], None, op0=ALU.mult),
                        waits=[("SCE", c6b)], inc="VEC")
            c8 = P.emit("vector", lambda v: v.tensor_reduce(r_[0:1, M1:M1 + 1], r_[0:1, PR:PR + ne], axis=AXX, op=ALU.max),
                        waits=[("VEC", c7)], inc="VEC")
            c8b = P.emit("scalar", lambda s: s.activation(r_[0:1, M1 + 2:M1 + 3], r_[0:1, M1:M1 + 1], AF.Copy),
                         waits=[("VEC", c8)], inc="SCE")
            c9 = P.emit("vector", lambda v: v.tensor_scalar(r_[0:1, KP:KP + ne], r_[0:1, PR:PR + ne],
                                                            r_[0:1, M1 + 2:M1 + 3], None, op0=ALU.is_ge),
                        waits=[("SCE", c8b)], inc="VEC")
            c10 = P.emit("vector", lambda v: v.scalar_tensor_tensor(r_[0:1, PM:PM + ne], r_[0:1, KP:KP + ne], -1e30,
                                                                    r_[0:1, PR:PR + ne], op0=ALU.mult, op1=ALU.add),
                         waits=[("VEC", c9)], inc="VEC")
            c11 = P.emit("vector", lambda v: v.tensor_reduce(r_[0:1, M2:M2 + 1], r_[0:1, PM:PM + ne], axis=AXX, op=ALU.max),
                         waits=[("VEC", c10)], inc="VEC")
            c11b = P.emit("scalar", lambda s: s.activation(r_[0:1, M2 + 1:M2 + 2], r_[0:1, M2:M2 + 1], AF.Copy),
                          waits=[("VEC", c11)], inc="SCE")
            c12 = P.emit("vector", lambda v: v.tensor_scalar(r_[0:1, KP:KP + ne], r_[0:1, PR:PR + ne],
                                                             r_[0:1, M2 + 1:M2 + 2], None, op0=ALU.is_ge),
                         waits=[("SCE", c11b)], inc="VEC")
            c13 = P.emit("vector", lambda v: v.tensor_tensor(r_[0:1, WR:WR + ne], r_[0:1, PR:PR + ne],
                                                             r_[0:1, KP:KP + ne], op=ALU.mult),
                         waits=[("VEC", c12)], inc="VEC")
            c14 = P.emit("vector", lambda v: v.tensor_reduce(r_[0:1, M2:M2 + 1], r_[0:1, WR:WR + ne], axis=AXX, op=ALU.add),
                         waits=[("VEC", c13)], inc="VEC")
            c14b = P.emit("scalar", lambda s: s.activation(r_[0:1, M2 + 2:M2 + 3], r_[0:1, M2:M2 + 1], AF.Copy),
                          waits=[("VEC", c14)], inc="SCE")
            c15 = P.emit("vector", lambda v: v.reciprocal(r_[0:1, M2 + 1:M2 + 2], r_[0:1, M2 + 2:M2 + 3]),
                         waits=[("SCE", c14b)], inc="VEC")
            c15b = P.emit("scalar", lambda s: s.activation(r_[0:1, M2 + 3:M2 + 4], r_[0:1, M2 + 1:M2 + 2], AF.Copy),
                          waits=[("VEC", c15)], inc="SCE")
            c16 = P.emit("vector", lambda v: v.tensor_scalar(r_[0:1, WR:WR + ne], r_[0:1, WR:WR + ne],
                                                             r_[0:1, M2 + 3:M2 + 4], None, op0=ALU.mult),
                         waits=[("SCE", c15b)], inc="VEC")
            cwb = P.emit("tensor", lambda t: t.matmul(PS[:, 512:512 + ne], onesrf[:, :], r_[0:1, WR:WR + ne],
                                                      start=True, stop=True),
                         waits=[("VEC", c16)], inc="TEN")
            cwe = P.emit("vector", lambda v: v.tensor_copy(wec[:, :], PS[:, 512:512 + ne]),
                         waits=[("TEN", cwb)], inc="VEC")
            cwc = P.emit("tensor", lambda t: t.matmul(PS[0:8, 1024:1025], r_[0:1, WR:WR + ne], onesrf[0:1, 0:1],
                                                      start=True, stop=True),
                         waits=[("VEC", c16)], inc="TEN")
            cwc2 = P.emit("vector", lambda v: v.tensor_copy(wcolb[:, :], PS[0:8, 1024:1025]),
                          waits=[("TEN", cwc)], inc="VEC")
            cbe = None
            for half in range(2):
                cbe = P.emit("tensor",
                             lambda t, half=half: t.matmul(PS[0:1, 1536 + 512 * half:2048 + 512 * half],
                                                           wcolb[:, :], b2s[:, 512 * half:512 * half + 512],
                                                           start=True, stop=True),
                             waits=[("VEC", cwc2), ("GP", c_params)], inc="TEN")
            cbr = P.emit("vector", lambda v: v.tensor_copy(rowb[0:1, 0:1024], PS[0:1, 1536:2560]),
                         waits=[("TEN", cbe)], inc="VEC")
            cbb = None
            for half in range(2):
                cbb = P.emit("tensor",
                             lambda t, half=half: t.matmul(PS[:, 2560 + 512 * half:3072 + 512 * half],
                                                           onesrf[:, :], rowb[0:1, 512 * half:512 * half + 512],
                                                           start=True, stop=True),
                             waits=[("VEC", cbr)], inc="TEN")
            c_b2bc = P.emit("vector", lambda v: v.tensor_copy(b2bc[:, :], PS[:, 2560:3584]),
                            waits=[("TEN", cbb)], inc="VEC")

        NS1 = 8
        NS2 = 8
        w1slab_loaded = {}
        w1slab_freed = {}
        w2slab_loaded = {}
        w2slab_freed = {}
        c_ws_start = P.count("TEN")

        def load_w1slab(gs):
            pe, s_ = divmod(gs, NS1)
            p_, e = divmod(pe, ne)
            if p_ >= npair:
                return
            w = [(f"CE{e}", c_wexp[e])]
            if gs >= 2:
                w.append(("TEN", w1slab_freed[gs - 2]))
            else:
                w.append(("TEN", c_ws_start))
            cnt = P.emit(
                "sync",
                lambda s, e=e, s_=s_, gs=gs: s.dma_start(
                    w1r[:, gs % 2, :, :, :], sw1[e][:, 4 * s_:4 * s_ + 4, :, :]),
                waits=w, inc=f"W1{gs % 2}", inc_by=16)
            w1slab_loaded[gs] = (f"W1{gs % 2}", cnt)

        def load_w2slab(gs):
            pe, s_ = divmod(gs, NS2)
            p_, e = divmod(pe, ne)
            if p_ >= npair:
                return
            w = [(f"CE{e}", c_wexp[e])]
            if gs >= 2:
                w.append(("TEN", w2slab_freed[gs - 2]))
            else:
                w.append(("TEN", c_ws_start))
            cnt = P.emit(
                "sync",
                lambda s, e=e, s_=s_, gs=gs: s.dma_start(
                    w2r[:, gs % 2, :, :], sw2[e][:, 4 * s_:4 * s_ + 4, :]),
                waits=w, inc=f"W2{gs % 2}", inc_by=16)
            w2slab_loaded[gs] = (f"W2{gs % 2}", cnt)

        load_w1slab(0)
        load_w1slab(1)
        load_w2slab(0)
        load_w2slab(1)

        yev = {}
        gelu_cnt = {}
        prev_pair_ten = None
        prev_xT = 0
        c_mm1_last = {}
        pair_xT = {}

        def emit_pair_prefix(p_):
            nonlocal prev_xT
            t0 = 4 * p_
            apd = []
            for tt in range(4):
                w = []
                if tt == 0:
                    w = [("VEC", c_b2bc)]
                    if p_ > 0:
                        w.append(("TEN", c_mm1_last[p_ - 1]))
                    if prev_xT:
                        w.append(("TRP", prev_xT))
                cap = P.emit("vector",
                             lambda v, tt=tt, t0=t0: v.tensor_scalar(
                                 xln[:, tt, :], R[:, t0 + tt, :],
                                 mu2c[:, t0 + tt:t0 + tt + 1], rs2c[:, t0 + tt:t0 + tt + 1],
                                 op0=ALU.subtract, op1=ALU.mult),
                             waits=w, inc="VEC")
                apd.append(cap)
            for tt in range(4):
                P.emit("scalar",
                       lambda s, tt=tt: s.dma_start(
                           xlnT[:, :, tt * 128:tt * 128 + 128],
                           xln[:, tt, :].rearrange("p (k j) -> p k j", j=128),
                           transpose=True),
                       waits=[("VEC", apd[tt])], inc="TRP", inc_by=16)
            pair_xT[p_] = P.count("TRP")
            prev_xT = pair_xT[p_]

        emit_pair_prefix(0)
        for p_ in range(npair):
            t0 = 4 * p_
            c_xT = pair_xT[p_]

            for e in range(ne):
                ge = p_ * ne + e
                for m in range(32):
                    gs = ge * NS1 + m // 4
                    pbk = 4 + m % 4
                    w0 = [w1slab_loaded[gs], ("TRP", c_xT)]
                    gm = ge * 32 + m
                    if gm >= 4:
                        w0.append(("SCE", gelu_cnt[gm - 4]))
                    if ge > 0 and m < 4:
                        w0.append(yev[(ge - 1, 2 + m // 2)])
                    first = True
                    for k in range(8):
                        cm = P.emit("tensor",
                                    lambda t, m=m, k=k, pbk=pbk, gs=gs: t.matmul(
                                        bank(pbk),
                                        w1r[:, gs % 2, m % 4, k, :],
                                        xlnT[:, k, :],
                                        start=(k == 0), stop=(k == 7)),
                                    waits=w0 if first else [], inc="TEN")
                        first = False
                    if m % 4 == 3:
                        w1slab_freed[gs] = cm
                        load_w1slab(gs + 2)
                    if m == 31 and e == ne - 1:
                        c_mm1_last[p_] = cm
                    gelu_cnt[gm] = P.emit("scalar",
                                          lambda s, m=m, pbk=pbk, e=e: s.activation(
                                              qkvB[:, m, :], bank(pbk), AF.Gelu,
                                              bias=b1c[:, e, m:m + 1]),
                                          waits=[("TEN", cm)], inc="SCE")
                c_gelu_all = gelu_cnt[ge * 32 + 31]
                frw = {}
                for r in range(4):
                    frw[r] = [("SCE", c_gelu_all)]
                    if ge > 0:
                        frw[r].append(yev[(ge - 1, r)])
                first = {r: True for r in range(4)}
                for k2 in range(32):
                    gs2 = ge * NS2 + k2 // 4
                    wk = [w2slab_loaded[gs2]]
                    for r in range(4):
                        cp, tw = divmod(r, 2)
                        for half in range(2):
                            wts = []
                            if first[r]:
                                wts = frw[r] + wk
                            elif half == 0 and r == 0 and k2 % 4 == 0:
                                wts = wk
                            cm = P.emit("tensor",
                                        lambda t, k2=k2, cp=cp, tw=tw, half=half, r=r, gs2=gs2: t.matmul(
                                            bank(2 * r + half),
                                            qkvB[:, k2, cp * 256 + tw * 128: cp * 256 + tw * 128 + 128],
                                            w2r[:, gs2 % 2, k2 % 4, 512 * half:512 * half + 512],
                                            start=(k2 == 0), stop=(k2 == 31)),
                                        waits=wts, inc="TEN")
                            first[r] = False
                    if k2 % 4 == 3:
                        w2slab_freed[gs2] = cm
                        load_w2slab(gs2 + 2)
                c_mm2 = cm
                for r in (2, 3, 0, 1):
                    if e == 0:
                        cy = P.emit("vector",
                                    lambda v, r=r: v.tensor_scalar(
                                        ybuf[:, r, :], PS[:, 1024 * r:1024 * r + 1024],
                                        wec[:, 0:1], None, op0=ALU.mult),
                                    waits=[("TEN", c_mm2)], inc="VEC")
                    else:
                        cy = P.emit("vector",
                                    lambda v, r=r, e=e: v.scalar_tensor_tensor(
                                        ybuf[:, r, :], PS[:, 1024 * r:1024 * r + 1024],
                                        wec[:, e:e + 1], ybuf[:, r, :],
                                        op0=ALU.mult, op1=ALU.add),
                                    waits=[("TEN", c_mm2)], inc="VEC")
                    yev[(ge, r)] = ("VEC", cy)
            prev_pair_ten = P.count("TEN")
            if p_ + 1 < npair:
                emit_pair_prefix(p_ + 1)

            ge_last = p_ * ne + ne - 1
            for r in range(4):
                gi = 4 * p_ + r
                cb = P.emit("vector",
                            lambda v, r=r: v.tensor_tensor(ybuf[:, r, :], ybuf[:, r, :], b2bc[:, :], op=ALU.add),
                            waits=[yev[(ge_last, r)]], inc="VEC")
                cs1 = P.emit("vector", lambda v, r=r: v.reduce_sum(cst[:, 28:29], ybuf[:, r, :], axis=AXX),
                             waits=[("VEC", cb)], inc="VEC")
                cq1 = P.emit("scalar",
                             lambda s, r=r: s.activation(
                                 qkvB[:, 0:2, :], ybuf[:, r, :].rearrange("p (a b) -> p a b", a=2),
                                 AF.Square, accum_out=cst[:, 29:30]),
                             waits=[("VEC", cb), ("SCE", P.count("SCE"))], inc="SCE")
                cm1 = P.emit("vector", lambda v: v.tensor_scalar(cst[:, 28:29], cst[:, 28:29], 1.0 / D, None, op0=ALU.mult),
                             waits=[("VEC", cs1)], inc="VEC")
                cm2 = P.emit("vector", lambda v: v.tensor_tensor(cst[:, 30:31], cst[:, 28:29], cst[:, 28:29], op=ALU.mult),
                             waits=[("VEC", cm1)], inc="VEC")
                cm3 = P.emit("vector",
                             lambda v: v.scalar_tensor_tensor(cst[:, 30:31], cst[:, 29:30], 1.0 / D,
                                                              cst[:, 30:31], op0=ALU.mult, op1=ALU.subtract),
                             waits=[("SCE", cq1), ("VEC", cm2)], inc="VEC")
                cln = P.emit("scalar", lambda s: s.activation(cst[:, 31:32], cst[:, 30:31], AF.Ln, bias=epsc[:, 0:1]),
                             waits=[("VEC", cm3)], inc="SCE")
                crs = P.emit("scalar", lambda s: s.activation(cst[:, 30:31], cst[:, 31:32], AF.Exp, scale=-0.5),
                             waits=[("SCE", cln)], inc="SCE")
                wa = [("SCE", crs)]
                if gi >= 2:
                    wa.append((f"ST{gi % 2}", 16 * ((gi - 2) // 2 + 1)))
                ca1 = P.emit("vector",
                             lambda v, r=r, gi=gi: v.tensor_scalar(outb[:, gi % 2, :], ybuf[:, r, :],
                                                                   cst[:, 28:29], cst[:, 30:31],
                                                                   op0=ALU.subtract, op1=ALU.mult),
                             waits=wa, inc="VEC")
                ca2 = P.emit("vector",
                             lambda v, r=r, t0=t0, gi=gi: v.tensor_tensor(outb[:, gi % 2, :], outb[:, gi % 2, :],
                                                                          R[:, t0 + r, :], op=ALU.add),
                             inc="VEC")
                P.emit("gpsimd",
                       lambda g, t0=t0, r=r, gi=gi: g.dma_start(dout[(t0 + r) * 128:(t0 + r + 1) * 128, :],
                                                                outb[:, gi % 2, :]),
                       waits=[("VEC", ca2)], inc=f"ST{gi % 2}", inc_by=16)

        P.emit("gpsimd", lambda g: None, waits=[("ST0", P.count("ST0")), ("ST1", P.count("ST1"))])
        P.emit("sync", lambda s: None, waits=[("ST0", P.count("ST0")), ("ST1", P.count("ST1"))])

        def replay(eng, name):
            for fn, waits, inc, inc_by in P.ops[name]:
                for (s, v) in waits:
                    eng.wait_ge(sems[s], v)
                inst = fn(eng)
                if inc is not None and inst is not None:
                    inst.then_inc(sems[inc], inc_by)

        @block.sync
        def _(s):
            replay(s, "sync")

        @block.gpsimd
        def _(g):
            replay(g, "gpsimd")

        @block.tensor
        def _(t):
            replay(t, "tensor")

        @block.scalar
        def _(s):
            replay(s, "scalar")

        @block.vector
        def _(v):
            replay(v, "vector")

    return nc


def _prep_weights(inputs):
    Wqkv = np.ascontiguousarray(inputs["L0_attn_W_qkv"])
    Wproj = np.ascontiguousarray(inputs["L0_attn_W_proj"])
    w1 = np.ascontiguousarray(inputs["L0_ffn_w1"])
    w2 = np.ascontiguousarray(inputs["L0_ffn_w2"])
    b1 = np.ascontiguousarray(inputs["L0_ffn_b1"])
    b2 = np.ascontiguousarray(inputs["L0_ffn_b2"])
    gate = np.ascontiguousarray(inputs["L0_ffn_gate_w"])
    g_ = inputs["L0_attn_ln_gamma"]
    be = inputs["L0_attn_ln_beta"]
    bp = inputs["L0_attn_b_proj"]

    out = {}
    out["wqkv"] = np.ascontiguousarray(Wqkv.reshape(8, 128, 24, 128).transpose(1, 2, 0, 3))
    out["wproj"] = np.ascontiguousarray(Wproj.reshape(8, 128, D).transpose(1, 0, 2))
    out["w1"] = np.ascontiguousarray(w1.reshape(8, 8, 128, 32, 128).transpose(0, 2, 3, 1, 4))
    out["w2"] = np.ascontiguousarray(w2.reshape(8, 32, 128, D).transpose(0, 2, 1, 3))
    out["b1"] = np.ascontiguousarray(b1.reshape(8, 32, 128).transpose(0, 2, 1))
    out["b2"] = b2
    out["gate"] = np.ascontiguousarray(gate.reshape(8, 128, 8).transpose(1, 0, 2))
    out["gbc"] = np.ascontiguousarray(np.broadcast_to(np.tile(g_, H)[None, :], (128, D)))
    out["bbc"] = np.ascontiguousarray(np.broadcast_to(np.tile(be, H)[None, :], (128, D)))
    out["pbc"] = np.ascontiguousarray(np.broadcast_to(bp[None, :], (128, D)))
    rotm = np.zeros((128, 128), np.float32)
    for m in range(128):
        rotm[(m + 64) % 128, m] = 1.0
    out["rot"] = rotm
    out["onescol"] = np.ones((128, 1), np.float32)
    out["onesrow"] = np.ones((1, 128), np.float32)
    return out


def kernel(**inputs):
    x = inputs["L0_attn_x"]
    wb = _prep_weights(inputs)
    nc = build_nc()
    in_maps = []
    for c in range(8):
        m = {"x": np.ascontiguousarray(x[c])}
        m.update(wb)
        in_maps.append(m)
    res = run_bass_kernel_spmd(nc, in_maps, core_ids=list(range(8)))
    out = np.stack([res.results[c]["out"] for c in range(8)], axis=0)
    return out.astype(np.float32)
